# version 03
# baseline (speedup 1.0000x reference)
import numpy as np
from contextlib import ExitStack

import concourse.bass as bass
import concourse.mybir as mybir
from concourse.bass_utils import run_bass_kernel_spmd

F32 = mybir.dt.float32
BF16 = mybir.dt.bfloat16
I32 = mybir.dt.int32
AF = mybir.ActivationFunctionType
ALU = mybir.AluOpType
AX = mybir.AxisListType

TOK = 2048
NCH = 16
DEBUG_XQ = False
C = 2048
H = 16
FH = 8192
NI = 16


def build():
    nc = bass.Bass("TRN2", target_bir_lowering=False)

    def dri(n, s):
        return nc.dram_tensor(n, s, F32, kind="ExternalInput").ap()

    x_in = dri("x", [TOK, C])
    wqkv = dri("wqkv", [C, 3 * C])
    wg_a = dri("wg_a", [C, C])
    wproj = dri("wproj", [C, C])
    gnw_b = dri("gnw_b", [128, C])
    gnb_b = dri("gnb_b", [128, C])
    Wg = dri("Wg", [FH, C])
    Wu = dri("Wu", [FH, C])
    Wd = dri("Wd", [C, FH])
    cos_t = dri("cos_t", [TOK, C])
    nsin_t = dri("nsin_t", [TOK, C])
    dq_t = dri("dq_t", [128, C])
    dk_t = dri("dk_t", [128, C])
    m2_t = dri("m2_t", [128, C])
    gl_t = dri("gl_t", [128, C])
    gamA = dri("gamA", [4, 128, C])
    ident = dri("ident", [128, 128])
    out = nc.dram_tensor("out", [TOK, C], F32, kind="ExternalOutput").ap()

    def idr(n, s, dt):
        return nc.dram_tensor(n, s, dt, kind="Internal").ap()

    sgT_g = idr("sgT_g", [64, NI, 128, 128], BF16)
    sgT_u = idr("sgT_u", [64, NI, 128, 128], BF16)
    sgT_d = idr("sgT_d", [64, 128, C], BF16)
    xnsp = idr("xnsp", [NCH, NI, 128, 128], BF16)
    qhat = idr("qhat", [NCH, 128, C], BF16)
    khat = idr("khat", [NCH, 128, C], BF16)
    vsp = idr("vsp", [NCH, 128, C], BF16)
    gsp = idr("gsp", [NCH, 128, C], BF16)
    spf = idr("spf", [NCH, NI, 128, 128], BF16)
    gffn = idr("gffn", [NCH, 128, C], F32)
    dbg_xq = idr("dbg_xq", [NI, 128, TOK], BF16)
    cin = idr("cin", [128, C], F32)
    cout = idr("cout", [512, C], F32)

    es = ExitStack()
    E = es.enter_context
    identb = E(nc.sbuf_tensor("identb", [128, 128], BF16))
    eps5 = E(nc.sbuf_tensor("eps5", [128, 1], F32))
    eps8 = E(nc.sbuf_tensor("eps8", [128, 1], F32))
    PD4 = [E(nc.psum_tensor("PD%d" % j, [128, 512], F32)) for j in range(4)]
    PE1 = E(nc.psum_tensor("PE1", [128, 512], F32))
    PE2 = E(nc.psum_tensor("PE2", [128, 512], F32))
    PT1 = E(nc.psum_tensor("PT1", [128, 1024], BF16))
    PT2 = E(nc.psum_tensor("PT2", [128, 1024], BF16))
    PTs = (PT1, PT2)

    sq_ = E(nc.semaphore("sq"))
    gq = E(nc.semaphore("gq"))
    tv = E(nc.semaphore("tv"))
    vv = E(nc.semaphore("vv"))
    ssc = E(nc.semaphore("ssc"))
    bar = E(nc.semaphore("bar"))
    col = E(nc.semaphore("col"))
    _block_raw = E(nc.Block())

    cnt = {"sq": 0, "gq": 0, "tv": 0, "vv": 0, "ssc": 0, "col": 0}
    sem = {"sq": sq_, "gq": gq, "tv": tv, "vv": vv, "ssc": ssc, "col": col}

    _FENCED = {"tensor_tensor", "tensor_scalar", "tensor_scalar_mul",
               "tensor_scalar_add", "tensor_reduce", "tensor_copy",
               "reciprocal", "memset", "activation", "sign", "select"}

    class EngProxy:
        def __init__(self, eng, s):
            self._e, self._s = eng, s

        def __getattr__(self, nm):
            f = getattr(self._e, nm)
            if nm in _FENCED:
                def w(*a, **kw):
                    ii = f(*a, **kw)
                    ii.then_inc(sem[self._s], 1)
                    cnt[self._s] += 1
                    self._e.wait_ge(sem[self._s], cnt[self._s])
                    return ii
                return w
            return f

    class BlockProxy:
        def __init__(self, b):
            self._b = b

        def __getattr__(self, name):
            dec = getattr(self._b, name)
            if name == "vector":
                return lambda fn: dec(lambda eng: fn(EngProxy(eng, "vv")))
            if name == "scalar":
                return lambda fn: dec(lambda eng: fn(EngProxy(eng, "ssc")))
            return dec

    def inc(instr, s, n=None):
        if s in ("vv", "ssc"):
            return cnt[s]
        if n is None:
            n = 1
        instr.then_inc(sem[s], n)
        cnt[s] += n
        return cnt[s]

    block = BlockProxy(_block_raw)

    def nrecip(v, out, in_, tmp):
        v.reciprocal(out=out, in_=in_)
        v.tensor_tensor(out=tmp, in0=in_, in1=out, op=ALU.mult)
        v.tensor_scalar_mul(tmp, tmp, -1.0)
        v.tensor_scalar_add(tmp, tmp, 2.0)
        return v.tensor_tensor(out=out, in0=out, in1=tmp, op=ALU.mult)

    def dma(h, s, **kw):
        ii = h.dma_start(**kw)
        ii.then_inc(sem[s], 16)
        cnt[s] += 16
        h.wait_ge(sem[s], cnt[s])
        return cnt[s]

    ENGS = ("sync", "gpsimd", "scalar", "vector", "tensor")
    nbar = [0]

    def barrier():
        nbar[0] += 1
        tgt = nbar[0] * len(ENGS)
        for e in ENGS:
            @getattr(block, e)
            def _(h):
                h.sem_inc(bar, 1)
        for e in ENGS:
            @getattr(block, e)
            def _(h):
                h.wait_ge(bar, tgt)

    def transpose_seq(pairs, t_pre=None, v_pre=None):
        n = len(pairs)
        tbase = cnt["tv"]
        vbase = cnt["vv"]

        @block.tensor
        def _(t):
            if t_pre:
                for s, val in t_pre:
                    t.wait_ge(sem[s], val)
            for i, (src, dstp) in enumerate(pairs):
                if i >= 2:
                    t.wait_ge(vv, vbase + i - 1)
                else:
                    t.wait_ge(vv, vbase)
                inc(t.transpose(PTs[i % 2][:, 0:128], src, identb[:]), "tv")

        @block.vector
        def _(v):
            if v_pre:
                for s, val in v_pre:
                    v.wait_ge(sem[s], val)
            for i, (src, dstp) in enumerate(pairs):
                v.wait_ge(tv, tbase + i + 1)
                inc(v.tensor_copy(out=dstp, in_=PTs[i % 2][:, 0:128]), "vv")
        return cnt["vv"]

    @block.gpsimd
    def _(g):
        dma(g, "gq", out=identb[:], in_=ident)

    @block.vector
    def _(v):
        v.memset(eps5[:], 1e-5)
        v.memset(eps8[:], 1e-8)

    barrier()

    with nc.sbuf_tensor("p0w", [128, 2, C], F32) as p0w, \
         nc.sbuf_tensor("p0wd", [128, FH], F32) as p0wd, \
         nc.sbuf_tensor("p0s", [128, 2, C], BF16) as p0s, \
         nc.sbuf_tensor("p0sd", [128, FH], BF16) as p0sd, \
         nc.sbuf_tensor("p0st", [128, NI, 128], BF16) as p0st:

        ld_m, sg_m, tp_m, wr_m = {}, {}, {}, {}
        for wi, (Wsrc, dst) in enumerate(((Wg, sgT_g), (Wu, sgT_u))):
            for hc in range(64):
                it = wi * 64 + hc

                @block.sync
                def _(sy):
                    if it >= 2:
                        sy.wait_ge(ssc, sg_m[it - 2])
                    ld_m[it] = dma(sy, "sq", out=p0w[:, it % 2, :],
                                                in_=Wsrc[hc * 128:(hc + 1) * 128, :])

                @block.scalar
                def _(sc):
                    sc.wait_ge(sq_, ld_m[it])
                    if it >= 2:
                        sc.wait_ge(tv, tp_m[it - 2])
                    sg_m[it] = inc(sc.sign(out=p0s[:, it % 2, :],
                                           in_=p0w[:, it % 2, :]), "ssc")

                t_pre = [("ssc", sg_m[it])]
                if it >= 1:
                    t_pre.append(("gq", wr_m[it - 1]))
                transpose_seq(
                    [(p0s[:, it % 2, i * 128:i * 128 + 128], p0st[:, i, :])
                     for i in range(NI)], t_pre=t_pre)
                tp_m[it] = cnt["tv"]
                cpd = cnt["vv"]

                @block.gpsimd
                def _(g):
                    g.wait_ge(vv, cpd)
                    wr_m[it] = dma(g, "gq", out=dst[hc].rearrange("i p c -> p i c"),
                                               in_=p0st[:])

        dld, dsg = {}, {}
        with nc.sbuf_tensor("p0st2", [128, 2, 128], BF16) as p0st2:
            for dt_i in range(16):
                @block.sync
                def _(sy):
                    if dt_i >= 1:
                        sy.wait_ge(ssc, dsg[dt_i - 1])
                    dld[dt_i] = dma(sy, "sq", out=p0wd[:],
                                                 in_=Wd[dt_i * 128:(dt_i + 1) * 128, :])

                @block.scalar
                def _(sc):
                    sc.wait_ge(sq_, dld[dt_i])
                    if dt_i >= 1:
                        sc.wait_ge(tv, cnt["tv"])
                    dsg[dt_i] = inc(sc.sign(out=p0sd[:], in_=p0wd[:]), "ssc")

                tbase = cnt["tv"]
                vbase = cnt["vv"]
                gbase = cnt["gq"]

                @block.tensor
                def _(t):
                    t.wait_ge(ssc, dsg[dt_i])
                    for j in range(64):
                        if j >= 2:
                            t.wait_ge(vv, vbase + j - 1)
                        else:
                            t.wait_ge(vv, vbase)
                        inc(t.transpose(PTs[j % 2][:, 0:128],
                                        p0sd[:, j * 128:j * 128 + 128], identb[:]), "tv")

                @block.vector
                def _(v):
                    for j in range(64):
                        v.wait_ge(tv, tbase + j + 1)
                        if j >= 2:
                            v.wait_ge(gq, gbase + 16 * (j - 1))
                        else:
                            v.wait_ge(gq, gbase)
                        inc(v.tensor_copy(out=p0st2[:, j % 2, :],
                                          in_=PTs[j % 2][:, 0:128]), "vv")

                @block.gpsimd
                def _(g):
                    for j in range(64):
                        g.wait_ge(vv, vbase + j + 1)
                        dma(g, "gq", out=sgT_d[j, :, dt_i * 128:(dt_i + 1) * 128],
                                        in_=p0st2[:, j % 2, :])

    barrier()

    ph_xqf = ExitStack()
    xqf = ph_xqf.enter_context(nc.sbuf_tensor("xqf", [128, NI, TOK], BF16))
    with nc.sbuf_tensor("pax", [128, 2, C], F32) as pax, \
         nc.sbuf_tensor("ps1", [128, C], F32) as ps1, \
         nc.sbuf_tensor("ps2", [128, C], F32) as ps2, \
         nc.sbuf_tensor("ps3", [128, C], F32) as ps3, \
         nc.sbuf_tensor("pai", [128, C], I32) as pai, \
         nc.sbuf_tensor("paxn", [128, 2, C], BF16) as paxn, \
         nc.sbuf_tensor("paxq", [128, 2, C], BF16) as paxq, \
         nc.sbuf_tensor("panf", [128, NI, 128], BF16) as panf, \
         nc.sbuf_tensor("psc", [128, 40], F32) as psc:

        x_m, v1_m, v2_m, v3_m, s1_m, s2_m, tq_m, w_m = {}, {}, {}, {}, {}, {}, {}, {}
        for c in range(NCH):
            @block.sync
            def _(sy):
                if c >= 2:
                    sy.wait_ge(vv, v3_m[c - 2])
                x_m[c] = dma(sy, "sq", out=pax[:, c % 2, :],
                                          in_=x_in[c * 128:(c + 1) * 128, :])

            @block.vector
            def _(v):
                xb = pax[:, c % 2, :]
                v.wait_ge(sq_, x_m[c])
                if c >= 2:
                    v.wait_ge(tv, tq_m[c - 2])
                m = psc[:, 0:1]; va = psc[:, 4:5]
                v.tensor_reduce(out=m, in_=xb, axis=AX.X, op=ALU.add)
                v.tensor_scalar_mul(m, m, 1.0 / C)
                v.tensor_scalar(out=ps1[:], in0=xb, scalar1=m, scalar2=None, op0=ALU.subtract)
                v.tensor_tensor(out=ps2[:], in0=ps1[:], in1=ps1[:], op=ALU.mult)
                v.tensor_reduce(out=va, in_=ps2[:], axis=AX.X, op=ALU.add)
                v1_m[c] = inc(v.tensor_scalar_mul(va, va, 1.0 / C), "vv")

            @block.scalar
            def _(sc):
                sc.wait_ge(vv, v1_m[c])
                s1_m[c] = inc(sc.activation(out=psc[:, 4:5], in_=psc[:, 4:5],
                                            func=AF.Sqrt, bias=eps5[:]), "ssc")

            @block.vector
            def _(v):
                v.wait_ge(ssc, s1_m[c])
                rstd = psc[:, 8:9]; m2 = psc[:, 12:13]; v2 = psc[:, 16:17]; am = psc[:, 20:21]
                nrecip(v, rstd, psc[:, 4:5], psc[:, 36:37])
                v.tensor_scalar(out=paxn[:, c % 2, :], in0=ps1[:], scalar1=rstd,
                                scalar2=None, op0=ALU.mult)
                v.tensor_scalar(out=ps3[:], in0=ps1[:], scalar1=rstd, scalar2=None, op0=ALU.mult)
                v.tensor_reduce(out=m2, in_=ps3[:], axis=AX.X, op=ALU.add)
                v.tensor_scalar_mul(m2, m2, 1.0 / C)
                v.tensor_scalar(out=ps1[:], in0=ps3[:], scalar1=m2, scalar2=None, op0=ALU.subtract)
                v.tensor_tensor(out=ps2[:], in0=ps1[:], in1=ps1[:], op=ALU.mult)
                v.tensor_reduce(out=v2, in_=ps2[:], axis=AX.X, op=ALU.add)
                v.tensor_scalar_mul(v2, v2, 1.0 / C)
                v2_m[c] = inc(v.tensor_reduce(out=am, in_=ps1[:], axis=AX.X, op=ALU.max,
                                              apply_absolute_value=True), "vv")

            @block.scalar
            def _(sc):
                sc.wait_ge(vv, v2_m[c])
                s2_m[c] = inc(sc.activation(out=psc[:, 16:17], in_=psc[:, 16:17],
                                            func=AF.Sqrt, bias=eps8[:]), "ssc")

            @block.vector
            def _(v):
                v.wait_ge(ssc, s2_m[c])
                am = psc[:, 20:21]; rq = psc[:, 24:25]; dsc = psc[:, 28:29]; t8 = psc[:, 32:33]
                nrecip(v, rq, am, psc[:, 36:37])
                v.tensor_scalar_mul(rq, rq, 127.0)
                v.tensor_scalar(out=ps2[:], in0=ps1[:], scalar1=rq, scalar2=None, op0=ALU.mult)
                v.tensor_scalar_add(ps2[:], ps2[:], 12582912.0)
                v.tensor_scalar_add(ps2[:], ps2[:], -12582912.0)
                nrecip(v, t8, psc[:, 16:17], psc[:, 36:37])
                v.tensor_tensor(out=dsc, in0=am, in1=t8, op=ALU.mult)
                v.tensor_scalar_mul(dsc, dsc, 1.0 / 127.0)
                v3_m[c] = inc(v.tensor_scalar(out=paxq[:, c % 2, :], in0=ps2[:], scalar1=dsc,
                                              scalar2=None, op0=ALU.mult), "vv")

            pairs = []
            for i in range(NI):
                pairs.append((paxn[:, c % 2, i * 128:i * 128 + 128], panf[:, i, :]))
                pairs.append((paxq[:, c % 2, i * 128:i * 128 + 128],
                              xqf[:, i, c * 128:(c + 1) * 128]))
            t_pre = [("vv", v3_m[c])]
            if c >= 1:
                t_pre.append(("gq", w_m[c - 1]))
            cpd = transpose_seq(pairs, t_pre=t_pre)
            tq_m[c] = cnt["tv"]

            @block.gpsimd
            def _(g):
                g.wait_ge(vv, cpd)
                w_m[c] = dma(g, "gq", out=xnsp[c].rearrange("i p c -> p i c"),
                                         in_=panf[:])

    if DEBUG_XQ:
        @block.gpsimd
        def _(g):
            for i in range(NI):
                dma(g, "gq", out=dbg_xq[i], in_=xqf[:, i, :])

    barrier()

    NHW = 32

    with nc.sbuf_tensor("wgs", [128, 2, 2, NI, 128], BF16) as wgs, \
         nc.sbuf_tensor("wus", [128, 2, 2, NI, 128], BF16) as wus, \
         nc.sbuf_tensor("wds", [128, 2, 2, C], BF16) as wds, \
         nc.sbuf_tensor("p_sb", [128, 2, 2, 1024], BF16) as p_sb, \
         nc.sbuf_tensor("gbf", [128, 2, 512], BF16) as gbf, \
         nc.sbuf_tensor("dacc", [128, 8, C], F32) as dacc, \
         nc.sbuf_tensor("gff", [128, 2, C], F32) as gff:

        wg_m, wd_m, p_m = {}, {}, {}
        dlast, gulast, evlast = {}, {}, {}
        dev_m, gffw_m, gffv_m = {}, {}, {}
        last_silu = [None]
        last_ptt = [None]
        all_rounds = [(tc, half) for tc in range(8) for half in range(2)]

        def emit_drounds(key, rounds):
            st, hw = key
            par = hw % 2
            for (tc, half) in rounds:
                dk_ = (st, tc, half)

                @block.tensor
                def _(t):
                    t.wait_ge(vv, p_m[key])
                    t.wait_ge(sq_, wd_m[key])
                    if half in evlast:
                        t.wait_ge(vv, evlast[half])
                    if hw == 0 and st == 1:
                        t.wait_ge(gq, gffw_m[0 * 8 + tc])
                    for hc2 in range(2):
                        for dw in range(2):
                            o = half * 1024 + dw * 512
                            inc(t.matmul(PD4[half * 2 + dw][:],
                                         p_sb[:, par, hc2, tc * 128:tc * 128 + 128],
                                         wds[:, par, hc2, o:o + 512],
                                         start=(hc2 == 0), stop=(hc2 == 1)), "tv")
                    dlast[key] = cnt["tv"]
                dmm = cnt["tv"]

                @block.vector
                def _(v):
                    v.wait_ge(tv, dmm)
                    for k in range(2):
                        dst = dacc[:, tc, half * 1024 + k * 512:half * 1024 + k * 512 + 512]
                        if hw == 0:
                            ii = v.tensor_copy(out=dst, in_=PD4[half * 2 + k][:])
                        else:
                            ii = v.tensor_tensor(out=dst, in0=PD4[half * 2 + k][:],
                                                 in1=dst, op=ALU.add)
                    dev_m[dk_] = inc(ii, "vv")
                    evlast[half] = dev_m[dk_]

        def emit_gff(stq):
            for tc in range(8):
                @block.scalar
                def _(sc):
                    for half in range(2):
                        sc.wait_ge(vv, dev_m[(stq, tc, half)])
                    if stq * 8 + tc >= 2:
                        sc.wait_ge(gq, gffw_m[stq * 8 + tc - 2])
                    gffv_m[stq * 8 + tc] = inc(
                        sc.activation(out=gff[:, tc % 2, :], in_=dacc[:, tc, :],
                                      func=AF.Silu), "ssc")

                @block.gpsimd
                def _(g):
                    g.wait_ge(ssc, gffv_m[stq * 8 + tc])
                    gffw_m[stq * 8 + tc] = dma(
                        g, "gq", out=gffn[stq * 8 + tc], in_=gff[:, tc % 2, :])

        for st in range(2):
            for hw in range(NHW):
                par = hw % 2
                hwg = st * NHW + hw
                key = (st, hw)
                pkey = None if hwg == 0 else ((st, hw - 1) if hw > 0 else (st - 1, NHW - 1))
                p2key = None
                if hwg >= 2:
                    p2key = (st, hw - 2) if hw >= 2 else (st - 1, NHW - 2 + hw)

                @block.sync
                def _(sy):
                    if p2key is not None:
                        sy.wait_ge(tv, max(gulast[p2key], dlast[p2key]))
                    h0 = hw * 2
                    for d in range(2):
                        dma(sy, "sq", out=wgs[:, par, d, :, :],
                                         in_=sgT_g[h0 + d].rearrange("i p c -> p i c"))
                        dma(sy, "sq", out=wus[:, par, d, :, :],
                                         in_=sgT_u[h0 + d].rearrange("i p c -> p i c"))
                    wg_m[key] = cnt["sq"]
                    dma(sy, "sq", out=wds[:, par, :, :],
                                     in_=sgT_d[h0:h0 + 2].rearrange("h p c -> p h c"))
                    wd_m[key] = cnt["sq"]

                for gi, (hc, tw) in enumerate(((0, 0), (0, 1), (1, 0), (1, 1))):
                    sil_w = last_silu[0]
                    ptt_w = last_ptt[0]

                    @block.tensor
                    def _(t):
                        t.wait_ge(sq_, wg_m[key])
                        if sil_w is not None:
                            t.wait_ge(ssc, sil_w)
                        if ptt_w is not None:
                            t.wait_ge(vv, ptt_w)
                        for i in range(NI):
                            inc(t.matmul(PE1[:], wgs[:, par, hc, i, :],
                                         xqf[:, i, st * 1024 + tw * 512:st * 1024 + tw * 512 + 512],
                                         start=(i == 0), stop=(i == NI - 1)), "tv")
                        gm = cnt["tv"]
                        for i in range(NI):
                            inc(t.matmul(PE2[:], wus[:, par, hc, i, :],
                                         xqf[:, i, st * 1024 + tw * 512:st * 1024 + tw * 512 + 512],
                                         start=(i == 0), stop=(i == NI - 1)), "tv")
                        p_m[(key, gi, "g")] = gm
                        p_m[(key, gi, "u")] = cnt["tv"]
                    gulast[key] = cnt["tv"]

                    pt2 = last_ptt[0]

                    @block.scalar
                    def _(sc):
                        sc.wait_ge(tv, p_m[(key, gi, "g")])
                        if pt2 is not None:
                            sc.wait_ge(vv, pt2)
                        gffv = inc(sc.activation(out=gbf[:, gi % 2, :], in_=PE1[:],
                                                 func=AF.Silu), "ssc")
                        p_m[(key, gi, "s")] = gffv
                    last_silu[0] = cnt["ssc"]

                    @block.vector
                    def _(v):
                        v.wait_ge(tv, p_m[(key, gi, "u")])
                        v.wait_ge(ssc, p_m[(key, gi, "s")])
                        if p2key is not None and gi == 0:
                            v.wait_ge(tv, dlast[p2key])
                        pvv = inc(v.tensor_tensor(out=p_sb[:, par, hc, tw * 512:tw * 512 + 512],
                                                  in0=PE2[:], in1=gbf[:, gi % 2, :],
                                                  op=ALU.mult), "vv")
                        if gi == 3:
                            p_m[key] = pvv
                    last_ptt[0] = cnt["vv"]

                    if pkey is not None:
                        emit_drounds(pkey, all_rounds[gi * 4:(gi + 1) * 4])

                if st == 1 and hw == 0:
                    emit_gff(0)

        emit_drounds((1, NHW - 1), all_rounds)
        emit_gff(1)

    barrier()
    ph_xqf.close()

    ph_S = ExitStack()
    S_sb = ph_S.enter_context(nc.sbuf_tensor("S_sb", [128, C], F32))
    Sbf = ph_S.enter_context(nc.sbuf_tensor("Sbf", [128, C], BF16))
    cst = ph_S.enter_context(nc.sbuf_tensor("cst", [128, 6, C], F32))

    @block.sync
    def _(sy):
        dma(sy, "sq", out=cst[:, 0, :], in_=dq_t)
        dma(sy, "sq", out=cst[:, 1, :], in_=dk_t)
        dma(sy, "sq", out=cst[:, 2, :], in_=gl_t)
        dma(sy, "sq", out=cst[:, 3, :], in_=m2_t)
        dma(sy, "sq", out=cst[:, 4, :], in_=gnw_b)
        dma(sy, "sq", out=cst[:, 5, :], in_=gnb_b)

    @block.vector
    def _(v):
        inc(v.memset(S_sb[:], 0.0), "vv")

    with nc.sbuf_tensor("wbig", [128, NI, C], BF16) as wbig, \
         nc.sbuf_tensor("xnb4", [128, 2, NI, 128], BF16) as xnb4, \
         nc.sbuf_tensor("cosb", [128, C], F32) as cosb, \
         nc.sbuf_tensor("nsb", [128, C], F32) as nsb, \
         nc.sbuf_tensor("qa", [128, C], F32) as qa, \
         nc.sbuf_tensor("qb", [128, C], F32) as qb, \
         nc.sbuf_tensor("qout", [128, 2, C], BF16) as qout, \
         nc.sbuf_tensor("kb4", [128, 2, C], BF16) as kb4:

        def proj_pass(kind, wsrc_ap, dst):
            wld = {}

            @block.gpsimd
            def _(g):
                wld[0] = dma(g, "gq", out=wbig[:],
                                         in_=wsrc_ap.rearrange("(i p) c -> p i c", p=128))

            xn_m, tb_m, vw_m, sp_m, tab_m, kr_m, amm_m, sup_m = ({} for _ in range(8))
            for c in range(NCH):
                @block.sync
                def _(sy):
                    if c >= 2:
                        sy.wait_ge(tv, tb_m[(c - 2, 3)])
                    xn_m[c] = dma(sy, "sq", out=xnb4[:, c % 2, :, :],
                                               in_=xnsp[c].rearrange("i p c -> p i c"))
                    if kind in ("q", "k"):
                        if c >= 1:
                            sy.wait_ge(vv, vw_m[(c - 1, 3)])
                        dma(sy, "sq", out=cosb[:], in_=cos_t[c * 128:(c + 1) * 128, :])
                        dma(sy, "sq", out=nsb[:], in_=nsin_t[c * 128:(c + 1) * 128, :])
                        tab_m[c] = cnt["sq"]
                    if kind == "v":
                        if c >= 2:
                            sy.wait_ge(tv, amm_m[c - 2])
                        kr_m[c] = dma(sy, "sq", out=kb4[:, c % 2, :], in_=khat[c])

                fin_s = "ssc" if kind == "gate" else "vv"
                for w in range(4):
                    @block.tensor
                    def _(t):
                        t.wait_ge(gq, wld[0])
                        t.wait_ge(sq_, xn_m[c])
                        if c >= 1:
                            t.wait_ge(sem[fin_s], vw_m[(c - 1, w)])
                        for i in range(NI):
                            inc(t.matmul(PD4[w][:],
                                         xnb4[:, c % 2, i, :],
                                         wbig[:, i, w * 512:w * 512 + 512],
                                         start=(i == 0), stop=(i == NI - 1)), "tv")
                        tb_m[(c, w)] = cnt["tv"]

                    if kind in ("q", "k"):
                        @block.vector
                        def _(v):
                            v.wait_ge(tv, tb_m[(c, w)])
                            v.wait_ge(sq_, tab_m[c])
                            if c >= 2 and w == 0:
                                v.wait_ge(gq, sp_m[c - 2])
                                if kind == "k":
                                    pass
                            a = w * 512
                            sl = slice(a, a + 512)
                            v.tensor_tensor(out=qa[:, sl], in0=PD4[w][:], in1=cosb[:, sl],
                                            op=ALU.mult)
                            v.tensor_tensor(out=qb[:, a:a + 512:2],
                                            in0=PD4[w][:, 1:512:2],
                                            in1=nsb[:, a:a + 512:2], op=ALU.mult)
                            v.tensor_tensor(out=qb[:, a + 1:a + 512:2],
                                            in0=PD4[w][:, 0:512:2],
                                            in1=nsb[:, a + 1:a + 512:2], op=ALU.mult)
                            v.tensor_tensor(out=qa[:, sl], in0=qa[:, sl], in1=qb[:, sl],
                                            op=ALU.add)
                            dtab = cst[:, 0, sl] if kind == "q" else cst[:, 1, sl]
                            vw_m[(c, w)] = inc(v.tensor_tensor(out=qout[:, c % 2, sl],
                                                               in0=qa[:, sl], in1=dtab,
                                                               op=ALU.mult), "vv")
                    elif kind == "v":
                        @block.vector
                        def _(v):
                            v.wait_ge(tv, tb_m[(c, w)])
                            if c >= 2 and w == 0:
                                v.wait_ge(gq, sp_m[c - 2])
                                v.wait_ge(tv, amm_m[c - 1])
                            sl = slice(w * 512, w * 512 + 512)
                            vw_m[(c, w)] = inc(v.tensor_copy(out=qout[:, c % 2, sl],
                                                             in_=PD4[w][:]), "vv")
                    else:
                        @block.scalar
                        def _(sc):
                            sc.wait_ge(tv, tb_m[(c, w)])
                            if c >= 2 and w == 0:
                                sc.wait_ge(gq, sp_m[c - 2])
                            sl = slice(w * 512, w * 512 + 512)
                            vw_m[(c, w)] = inc(sc.activation(out=qout[:, c % 2, sl],
                                                             in_=PD4[w][:], func=AF.Silu),
                                               "ssc")

                @block.gpsimd
                def _(g):
                    g.wait_ge(sem[fin_s], vw_m[(c, 3)])
                    sp_m[c] = dma(g, "gq", out=dst[c], in_=qout[:, c % 2, :])

                if kind == "v":
                    for r in range(2):
                        @block.tensor
                        def _(t):
                            t.wait_ge(sq_, kr_m[c])
                            t.wait_ge(vv, vw_m[(c, 3)])
                            if r == 1:
                                t.wait_ge(vv, sup_m[(c, 0)])
                            elif c >= 1:
                                t.wait_ge(vv, sup_m[c - 1])
                            for hh in range(r * 8, r * 8 + 8):
                                pe = PE1 if (hh // 4) % 2 == 0 else PE2
                                inc(t.matmul(pe[:, (hh % 4) * 128:(hh % 4) * 128 + 128],
                                             kb4[:, c % 2, hh * 128:hh * 128 + 128],
                                             qout[:, c % 2, hh * 128:hh * 128 + 128],
                                             start=True, stop=True), "tv")
                            amm_m[(c, r)] = cnt["tv"]

                        @block.vector
                        def _(v):
                            v.wait_ge(tv, amm_m[(c, r)])
                            for gg in (r * 2, r * 2 + 1):
                                sl = slice(gg * 512, gg * 512 + 512)
                                pe = PE1 if gg % 2 == 0 else PE2
                                v.tensor_tensor(out=S_sb[:, sl], in0=S_sb[:, sl],
                                                in1=cst[:, 2, sl], op=ALU.mult)
                                inc(v.tensor_tensor(out=S_sb[:, sl], in0=S_sb[:, sl],
                                                    in1=pe[:], op=ALU.add), "vv")
                            sup_m[(c, r)] = cnt["vv"]
                    amm_m[c] = amm_m[(c, 1)]
                    sup_m[c] = sup_m[(c, 1)]
            return sp_m, sup_m

        proj_pass("q", wqkv[:, 0:C], qhat)
        barrier()
        proj_pass("k", wqkv[:, C:2 * C], khat)
        barrier()
        _, supv = proj_pass("v", wqkv[:, 2 * C:3 * C], vsp)

        @block.gpsimd
        def _(g):
            g.wait_ge(vv, supv[NCH - 1])
            dma(g, "gq", out=cin, in_=S_sb[:])
            g.wait_ge(gq, cnt["gq"])
            inc(g.collective_compute("AllGather", ALU.bypass, ins=[cin], outs=[cout],
                                     replica_groups=[[0, 1, 2, 3], [4, 5, 6, 7]]), "col")

        barrier()
        proj_pass("gate", wg_a, gsp)
        barrier()

    with nc.sbuf_tensor("agb", [128, 4, C], F32) as agb, \
         nc.sbuf_tensor("gmb", [128, 4, C], F32) as gmb:
        @block.sync
        def _(sy):
            sy.wait_ge(col, cnt["col"])
            dma(sy, "sq", out=agb[:], in_=cout.rearrange("(i p) c -> p i c", p=128))
            dma(sy, "sq", out=gmb[:], in_=gamA.rearrange("i p c -> p i c"))

        @block.vector
        def _(v):
            v.wait_ge(sq_, cnt["sq"])
            v.tensor_tensor(out=S_sb[:], in0=agb[:, 0, :], in1=gmb[:, 0, :], op=ALU.mult)
            for i in range(1, 4):
                v.tensor_tensor(out=agb[:, i, :], in0=agb[:, i, :], in1=gmb[:, i, :],
                                op=ALU.mult)
                v.tensor_tensor(out=S_sb[:], in0=S_sb[:], in1=agb[:, i, :], op=ALU.add)
            inc(v.tensor_copy(out=Sbf[:], in_=S_sb[:]), "vv")
    barrier()

    with nc.sbuf_tensor("qb4", [128, 2, C], BF16) as qb4, \
         nc.sbuf_tensor("kb4b", [128, 2, C], BF16) as kb4b, \
         nc.sbuf_tensor("vb4", [128, 2, C], BF16) as vb4, \
         nc.sbuf_tensor("gb4", [128, 2, C], BF16) as gb4, \
         nc.sbuf_tensor("qf", [128, H, 128], BF16) as qf, \
         nc.sbuf_tensor("kf", [128, H, 128], BF16) as kf, \
         nc.sbuf_tensor("scm", [128, C], BF16) as scm, \
         nc.sbuf_tensor("ynb", [128, C], F32) as ynb, \
         nc.sbuf_tensor("sqsb", [128, C], F32) as sqsb, \
         nc.sbuf_tensor("spb", [128, C], BF16) as spb, \
         nc.sbuf_tensor("spst", [128, NI, 128], BF16) as spst, \
         nc.sbuf_tensor("gnt", [128, 6, 16], F32) as gnt:

        ld_m, sc_m, ymm_m, amm_m, sup_m, gn_m, spw_m, tsp_m = ({} for _ in range(8))
        for c in range(NCH):
            @block.sync
            def _(sy):
                if c >= 2:
                    sy.wait_ge(tv, tsp_m[c - 2])
                dma(sy, "sq", out=qb4[:, c % 2, :], in_=qhat[c])
                dma(sy, "sq", out=kb4b[:, c % 2, :], in_=khat[c])
                dma(sy, "sq", out=vb4[:, c % 2, :], in_=vsp[c])
                dma(sy, "sq", out=gb4[:, c % 2, :], in_=gsp[c])
                ld_m[c] = cnt["sq"]

            pairs = []
            for hh in range(H):
                pairs.append((qb4[:, c % 2, hh * 128:hh * 128 + 128], qf[:, hh, :]))
                pairs.append((kb4b[:, c % 2, hh * 128:hh * 128 + 128], kf[:, hh, :]))
            t_pre = [("sq", ld_m[c])]
            cpd = transpose_seq(pairs, t_pre=t_pre)
            tr_cp = cpd

            for hh in range(H):
                @block.tensor
                def _(t):
                    t.wait_ge(vv, tr_cp)
                    pe = PE1 if hh % 2 == 0 else PE2
                    if hh >= 2:
                        t.wait_ge(vv, sc_m[(c, hh - 2)])
                    elif c >= 1:
                        t.wait_ge(vv, sup_m[c - 1])
                    inc(t.matmul(pe[:, 0:128], kf[:, hh, :], qf[:, hh, :],
                                 start=True, stop=True), "tv")
                    sc_m[(c, hh, "t")] = cnt["tv"]

                @block.vector
                def _(v):
                    v.wait_ge(tv, sc_m[(c, hh, "t")])
                    pe = PE1 if hh % 2 == 0 else PE2
                    sc_m[(c, hh)] = inc(
                        v.tensor_tensor(out=scm[:, hh * 128:hh * 128 + 128], in0=pe[:, 0:128],
                                        in1=cst[:, 3, hh * 128:hh * 128 + 128], op=ALU.mult),
                        "vv")

                @block.tensor
                def _(t):
                    t.wait_ge(vv, sc_m[(c, hh)])
                    if hh == 0 and c >= 1:
                        t.wait_ge(vv, gn_m[c - 1])
                    sl = slice(hh * 128, hh * 128 + 128)
                    psl = PD4[hh // 4][:, (hh % 4) * 128:(hh % 4) * 128 + 128]
                    inc(t.matmul(psl, qf[:, hh, :], Sbf[:, sl],
                                 start=True, stop=False), "tv")
                    inc(t.matmul(psl, scm[:, sl], vb4[:, c % 2, sl],
                                 start=False, stop=True), "tv")
                    ymm_m[(c, hh)] = cnt["tv"]

            for r in range(2):
                @block.tensor
                def _(t):
                    t.wait_ge(vv, sc_m[(c, H - 1)])
                    if r == 1:
                        t.wait_ge(vv, sup_m[(c, 0)])
                    for hh in range(r * 8, r * 8 + 8):
                        pe = PE1 if (hh // 4) % 2 == 0 else PE2
                        inc(t.matmul(pe[:, (hh % 4) * 128:(hh % 4) * 128 + 128],
                                     kb4b[:, c % 2, hh * 128:hh * 128 + 128],
                                     vb4[:, c % 2, hh * 128:hh * 128 + 128],
                                     start=True, stop=True), "tv")
                    amm_m[(c, r)] = cnt["tv"]

                @block.vector
                def _(v):
                    v.wait_ge(tv, amm_m[(c, r)])
                    if r == 0:
                        v.wait_ge(tv, ymm_m[(c, H - 1)])
                    for gg in (r * 2, r * 2 + 1):
                        sl = slice(gg * 512, gg * 512 + 512)
                        pe = PE1 if gg % 2 == 0 else PE2
                        v.tensor_tensor(out=S_sb[:, sl], in0=S_sb[:, sl],
                                        in1=cst[:, 2, sl], op=ALU.mult)
                        v.tensor_tensor(out=S_sb[:, sl], in0=S_sb[:, sl], in1=pe[:],
                                        op=ALU.add)
                        inc(v.tensor_copy(out=Sbf[:, sl], in_=S_sb[:, sl]), "vv")
                    sup_m[(c, r)] = cnt["vv"]
            sup_m[c] = sup_m[(c, 1)]

            @block.scalar
            def _(sc):
                sc.wait_ge(tv, ymm_m[(c, H - 1)])
                for j in range(4):
                    sc.activation(out=sqsb[:, j * 512:j * 512 + 512], in_=PD4[j][:],
                                  func=AF.Square)
                gn_m[(c, "sq")] = cnt["ssc"]

            @block.vector
            def _(v):
                v.wait_ge(tv, ymm_m[(c, H - 1)])
                mt = gnt[:, 0, :]; st16 = gnt[:, 1, :]; var16 = gnt[:, 2, :]
                for hh in range(H):
                    v.tensor_reduce(out=mt[:, hh:hh + 1],
                                    in_=PD4[hh // 4][:, (hh % 4) * 128:(hh % 4) * 128 + 128],
                                    axis=AX.X, op=ALU.add)
                v.tensor_scalar_mul(mt, mt, 1.0 / 128.0)
                v.wait_ge(ssc, gn_m[(c, "sq")])
                for hh in range(H):
                    v.tensor_reduce(out=st16[:, hh:hh + 1],
                                    in_=sqsb[:, hh * 128:hh * 128 + 128],
                                    axis=AX.X, op=ALU.add)
                v.tensor_scalar_mul(st16, st16, 1.0 / 128.0)
                v.tensor_tensor(out=var16, in0=mt, in1=mt, op=ALU.mult)
                gn_m[(c, "v")] = inc(v.tensor_tensor(out=var16, in0=st16, in1=var16,
                                                     op=ALU.subtract), "vv")

            @block.scalar
            def _(sc):
                sc.wait_ge(vv, gn_m[(c, "v")])
                gn_m[(c, "s")] = inc(sc.activation(out=gnt[:, 2, :], in_=gnt[:, 2, :],
                                                   func=AF.Sqrt, bias=eps5[:]), "ssc")

            @block.vector
            def _(v):
                v.wait_ge(ssc, gn_m[(c, "s")])
                mt = gnt[:, 0, :]; rst = gnt[:, 3, :]
                nrecip(v, rst, gnt[:, 2, :], gnt[:, 4, :])
                for hh in range(H):
                    v.tensor_scalar(out=ynb[:, hh * 128:hh * 128 + 128],
                                    in0=PD4[hh // 4][:, (hh % 4) * 128:(hh % 4) * 128 + 128],
                                    scalar1=mt[:, hh:hh + 1], scalar2=rst[:, hh:hh + 1],
                                    op0=ALU.subtract, op1=ALU.mult)
                v.tensor_tensor(out=ynb[:], in0=ynb[:], in1=cst[:, 4, :], op=ALU.mult)
                v.tensor_tensor(out=ynb[:], in0=ynb[:], in1=cst[:, 5, :], op=ALU.add)
                if c >= 1:
                    v.wait_ge(tv, tsp_m[c - 1])
                gn_m[c] = inc(v.tensor_tensor(out=spb[:], in0=ynb[:], in1=gb4[:, c % 2, :],
                                              op=ALU.mult), "vv")

            v_pre = [("gq", spw_m[c - 1])] if c >= 1 else None
            cpd2 = transpose_seq(
                [(spb[:, i * 128:i * 128 + 128], spst[:, i, :]) for i in range(NI)],
                t_pre=[("vv", gn_m[c])], v_pre=v_pre)
            tsp_m[c] = cnt["tv"]

            @block.gpsimd
            def _(g):
                g.wait_ge(vv, cpd2)
                spw_m[c] = dma(g, "gq", out=spf[c].rearrange("i p c -> p i c"),
                                           in_=spst[:])

    barrier()
    ph_S.close()

    with nc.sbuf_tensor("wpb", [128, NI, C], BF16) as wpb, \
         nc.sbuf_tensor("spfb", [128, 2, NI, 128], BF16) as spfb, \
         nc.sbuf_tensor("xb4", [128, 2, C], F32) as xb4, \
         nc.sbuf_tensor("gfb", [128, 2, C], F32) as gfb, \
         nc.sbuf_tensor("tmb", [128, C], F32) as tmb, \
         nc.sbuf_tensor("outb", [128, 2, C], F32) as outb:

        wld = {}

        @block.gpsimd
        def _(g):
            wld[0] = dma(g, "gq", out=wpb[:],
                                     in_=wproj.rearrange("(i p) c -> p i c", p=128))

        ld_m, mm_m, cb_m, ow_m = {}, {}, {}, {}
        for c in range(NCH):
            @block.sync
            def _(sy):
                if c >= 2:
                    sy.wait_ge(vv, cb_m[(c - 2, 3)])
                    sy.wait_ge(tv, mm_m[(c - 2, 3)])
                dma(sy, "sq", out=spfb[:, c % 2, :, :],
                                 in_=spf[c].rearrange("i p c -> p i c"))
                dma(sy, "sq", out=xb4[:, c % 2, :], in_=x_in[c * 128:(c + 1) * 128, :])
                dma(sy, "sq", out=gfb[:, c % 2, :], in_=gffn[c])
                ld_m[c] = cnt["sq"]

            for w in range(4):
                @block.tensor
                def _(t):
                    t.wait_ge(gq, wld[0])
                    t.wait_ge(sq_, ld_m[c])
                    if c >= 1:
                        t.wait_ge(vv, cb_m[(c - 1, w)])
                    for i in range(NI):
                        inc(t.matmul(PD4[w][:],
                                     spfb[:, c % 2, i, :],
                                     wpb[:, i, w * 512:w * 512 + 512],
                                     start=(i == 0), stop=(i == NI - 1)), "tv")
                    mm_m[(c, w)] = cnt["tv"]

                @block.vector
                def _(v):
                    v.wait_ge(tv, mm_m[(c, w)])
                    if c >= 2 and w == 0:
                        v.wait_ge(gq, ow_m[c - 2])
                    sl = slice(w * 512, w * 512 + 512)
                    v.tensor_tensor(out=tmb[:, sl], in0=PD4[w][:], in1=gfb[:, c % 2, sl],
                                    op=ALU.mult)
                    cb_m[(c, w)] = inc(v.tensor_tensor(out=outb[:, c % 2, sl],
                                                       in0=tmb[:, sl],
                                                       in1=xb4[:, c % 2, sl], op=ALU.add),
                                       "vv")

            @block.gpsimd
            def _(g):
                g.wait_ge(vv, cb_m[(c, 3)])
                ow_m[c] = dma(g, "gq", out=out[c * 128:(c + 1) * 128, :],
                                          in_=outb[:, c % 2, :])

        @block.gpsimd
        def _(g):
            g.wait_ge(gq, ow_m[NCH - 1])

    es.close()
    return nc


def _tables(core):
    f8 = np.float64
    j = core % 4
    base = j * TOK
    i2 = np.arange(0, C, 2, dtype=f8)
    dim_scale = (i2 + 0.4 * C) / (1.4 * C)
    inv_freq = 1.0 / 10000.0 ** (i2 / C)
    pos = base + np.arange(TOK, dtype=f8)
    scale = dim_scale[None, :] ** (pos[:, None] / 512.0)
    sinu = pos[:, None] * inv_freq[None, :]
    sin = np.sin(sinu) * scale
    cos = np.cos(sinu) * scale
    cosf = np.empty((TOK, C), f8)
    cosf[:, 0::2] = cos
    cosf[:, 1::2] = cos
    nsinf = np.empty((TOK, C), f8)
    nsinf[:, 0::2] = -sin
    nsinf[:, 1::2] = sin

    gam = 1.0 - 2.0 ** (-5.0 - np.arange(H, dtype=f8))
    t_loc = np.arange(128, dtype=f8)
    dq = np.empty((128, C), f8)
    dk = np.empty((128, C), f8)
    m2 = np.zeros((128, C), f8)
    gl = np.empty((128, C), f8)
    tri = (t_loc[None, :] >= t_loc[:, None]).astype(f8)
    for h in range(H):
        sl = slice(h * 128, (h + 1) * 128)
        dq[:, sl] = (gam[h] ** t_loc)[:, None] * (128.0 ** -0.5)
        dk[:, sl] = (gam[h] ** (128.0 - t_loc))[:, None]
        m2[:, sl] = tri * gam[h] ** (-128.0)
        gl[:, sl] = gam[h] ** 128.0
    ga = np.zeros((4, 128, C), f8)
    for i in range(4):
        if i < j:
            for h in range(H):
                ga[i, :, h * 128:(h + 1) * 128] = gam[h] ** (2048.0 * (j - 1 - i))
    f = np.float32
    return (cosf.astype(f), nsinf.astype(f), dq.astype(f), dk.astype(f),
            m2.astype(f), gl.astype(f), ga.astype(f))


def kernel(L0_attn_x, L0_attn_w_qkv, L0_attn_w_g, L0_attn_w_proj,
           L0_attn_gn_weight, L0_attn_gn_bias, L0_ffn_W_g, L0_ffn_W_u, L0_ffn_W_d):
    x = np.ascontiguousarray(np.asarray(L0_attn_x, dtype=np.float32))
    B, T, _ = x.shape
    gnw = np.ascontiguousarray(np.broadcast_to(
        np.asarray(L0_attn_gn_weight, np.float32)[None, :], (128, C)))
    gnb = np.ascontiguousarray(np.broadcast_to(
        np.asarray(L0_attn_gn_bias, np.float32)[None, :], (128, C)))
    identm = np.eye(128, dtype=np.float32)
    wqkv = np.ascontiguousarray(np.asarray(L0_attn_w_qkv, np.float32))
    wg_a = np.ascontiguousarray(np.asarray(L0_attn_w_g, np.float32))
    wproj = np.ascontiguousarray(np.asarray(L0_attn_w_proj, np.float32))
    Wg = np.ascontiguousarray(np.asarray(L0_ffn_W_g, np.float32))
    Wu = np.ascontiguousarray(np.asarray(L0_ffn_W_u, np.float32))
    Wd = np.ascontiguousarray(np.asarray(L0_ffn_W_d, np.float32))

    nc = build()
    in_maps = []
    for core in range(8):
        b = core // 4
        jj = core % 4
        cosf, nsinf, dq, dk, m2, gl, ga = _tables(core)
        xs = np.ascontiguousarray(x[b, jj * TOK:(jj + 1) * TOK, :])
        in_maps.append(dict(
            x=xs, wqkv=wqkv, wg_a=wg_a, wproj=wproj, gnw_b=gnw, gnb_b=gnb,
            Wg=Wg, Wu=Wu, Wd=Wd, cos_t=cosf, nsin_t=nsinf, dq_t=dq, dk_t=dk,
            m2_t=m2, gl_t=gl, gamA=ga, ident=identm))
    res = run_bass_kernel_spmd(nc, in_maps, core_ids=list(range(8)))
    outf = np.empty((B, T, C), np.float32)
    for core in range(8):
        b = core // 4
        jj = core % 4
        outf[b, jj * TOK:(jj + 1) * TOK, :] = res.results[core]["out"]
    return outf


# version 04
# speedup vs baseline: 1.3823x; 1.3823x over previous
import numpy as np
from contextlib import ExitStack

import concourse.bass as bass
import concourse.mybir as mybir
from concourse.bass_utils import run_bass_kernel_spmd

F32 = mybir.dt.float32
BF16 = mybir.dt.bfloat16
I32 = mybir.dt.int32
AF = mybir.ActivationFunctionType
ALU = mybir.AluOpType
AX = mybir.AxisListType

TOK = 2048
NCH = 16
DEBUG_XQ = False
C = 2048
H = 16
FH = 8192
NI = 16


def build():
    nc = bass.Bass("TRN2", target_bir_lowering=False)

    def dri(n, s):
        return nc.dram_tensor(n, s, F32, kind="ExternalInput").ap()

    x_in = dri("x", [TOK, C])
    wqkv = dri("wqkv", [C, 3 * C])
    wg_a = dri("wg_a", [C, C])
    wproj = dri("wproj", [C, C])
    gnw_b = dri("gnw_b", [128, C])
    gnb_b = dri("gnb_b", [128, C])
    Wgs = dri("Wgs", [FH // 8, C])
    Wus = dri("Wus", [FH // 8, C])
    Wds = dri("Wds", [C, FH // 8])
    cos_t = dri("cos_t", [TOK, C])
    nsin_t = dri("nsin_t", [TOK, C])
    dq_t = dri("dq_t", [128, C])
    dk_t = dri("dk_t", [128, C])
    m2_t = dri("m2_t", [128, C])
    gl_t = dri("gl_t", [128, C])
    gamA = dri("gamA", [4, 128, C])
    ident = dri("ident", [128, 128])
    out = nc.dram_tensor("out", [TOK, C], F32, kind="ExternalOutput").ap()

    def idr(n, s, dt, space="Local"):
        return nc.dram_tensor(n, s, dt, kind="Internal", addr_space=space).ap()

    sgT_g = idr("sgT_g", [64, NI, 128, 128], BF16, "Shared")
    sgT_u = idr("sgT_u", [64, NI, 128, 128], BF16, "Shared")
    sgT_d = idr("sgT_d", [64, 128, C], BF16, "Shared")
    cin_g = idr("cin_g", [8, NI, 128, 128], BF16)
    cin_u = idr("cin_u", [8, NI, 128, 128], BF16)
    cin_d = idr("cin_d", [8, 128, C], BF16)
    xnsp = idr("xnsp", [NCH, NI, 128, 128], BF16)
    qhat = idr("qhat", [NCH, 128, C], BF16)
    khat = idr("khat", [NCH, 128, C], BF16)
    vsp = idr("vsp", [NCH, 128, C], BF16)
    gsp = idr("gsp", [NCH, 128, C], BF16)
    spf = idr("spf", [NCH, NI, 128, 128], BF16)
    gffn = idr("gffn", [NCH, 128, C], F32)
    dbg_xq = idr("dbg_xq", [NI, 128, TOK], BF16)
    cin = idr("cin", [128, C], F32)
    cout = idr("cout", [512, C], F32)

    es = ExitStack()
    E = es.enter_context
    identb = E(nc.sbuf_tensor("identb", [128, 128], BF16))
    eps5 = E(nc.sbuf_tensor("eps5", [128, 1], F32))
    eps8 = E(nc.sbuf_tensor("eps8", [128, 1], F32))
    PD4 = [E(nc.psum_tensor("PD%d" % j, [128, 512], F32)) for j in range(4)]
    PE1 = E(nc.psum_tensor("PE1", [128, 512], F32))
    PE2 = E(nc.psum_tensor("PE2", [128, 512], F32))
    PT1 = E(nc.psum_tensor("PT1", [128, 1024], BF16))
    PT2 = E(nc.psum_tensor("PT2", [128, 1024], BF16))
    PTs = (PT1, PT2)

    sq_ = E(nc.semaphore("sq"))
    gq = E(nc.semaphore("gq"))
    tv = E(nc.semaphore("tv"))
    vv = E(nc.semaphore("vv"))
    ssc = E(nc.semaphore("ssc"))
    bar = E(nc.semaphore("bar"))
    col = E(nc.semaphore("col"))
    _block_raw = E(nc.Block())

    cnt = {"sq": 0, "gq": 0, "tv": 0, "vv": 0, "ssc": 0, "col": 0}
    sem = {"sq": sq_, "gq": gq, "tv": tv, "vv": vv, "ssc": ssc, "col": col}

    _FENCED = {"tensor_tensor", "tensor_scalar", "tensor_scalar_mul",
               "tensor_scalar_add", "tensor_reduce", "tensor_copy",
               "reciprocal", "memset", "activation", "sign", "select"}

    class EngProxy:
        def __init__(self, eng, s):
            self._e, self._s = eng, s

        def __getattr__(self, nm):
            f = getattr(self._e, nm)
            if nm in _FENCED:
                def w(*a, **kw):
                    ii = f(*a, **kw)
                    ii.then_inc(sem[self._s], 1)
                    cnt[self._s] += 1
                    self._e.wait_ge(sem[self._s], cnt[self._s])
                    return ii
                return w
            return f

    class BlockProxy:
        def __init__(self, b):
            self._b = b

        def __getattr__(self, name):
            dec = getattr(self._b, name)
            if name == "vector":
                return lambda fn: dec(lambda eng: fn(EngProxy(eng, "vv")))
            if name == "scalar":
                return lambda fn: dec(lambda eng: fn(EngProxy(eng, "ssc")))
            return dec

    def inc(instr, s, n=None):
        if s in ("vv", "ssc"):
            return cnt[s]
        if n is None:
            n = 1
        instr.then_inc(sem[s], n)
        cnt[s] += n
        return cnt[s]

    block = BlockProxy(_block_raw)

    def nrecip(v, out, in_, tmp):
        v.reciprocal(out=out, in_=in_)
        v.tensor_tensor(out=tmp, in0=in_, in1=out, op=ALU.mult)
        v.tensor_scalar_mul(tmp, tmp, -1.0)
        v.tensor_scalar_add(tmp, tmp, 2.0)
        return v.tensor_tensor(out=out, in0=out, in1=tmp, op=ALU.mult)

    def dma(h, s, **kw):
        ii = h.dma_start(**kw)
        ii.then_inc(sem[s], 16)
        cnt[s] += 16
        h.wait_ge(sem[s], cnt[s])
        return cnt[s]

    ENGS = ("sync", "gpsimd", "scalar", "vector", "tensor")
    nbar = [0]

    def barrier():
        nbar[0] += 1
        tgt = nbar[0] * len(ENGS)
        for e in ENGS:
            @getattr(block, e)
            def _(h):
                h.sem_inc(bar, 1)
        for e in ENGS:
            @getattr(block, e)
            def _(h):
                h.wait_ge(bar, tgt)

    def transpose_seq(pairs, t_pre=None, v_pre=None):
        n = len(pairs)
        tbase = cnt["tv"]
        vbase = cnt["vv"]

        @block.tensor
        def _(t):
            if t_pre:
                for s, val in t_pre:
                    t.wait_ge(sem[s], val)
            for i, (src, dstp) in enumerate(pairs):
                if i >= 2:
                    t.wait_ge(vv, vbase + i - 1)
                else:
                    t.wait_ge(vv, vbase)
                inc(t.transpose(PTs[i % 2][:, 0:128], src, identb[:]), "tv")

        @block.vector
        def _(v):
            if v_pre:
                for s, val in v_pre:
                    v.wait_ge(sem[s], val)
            for i, (src, dstp) in enumerate(pairs):
                v.wait_ge(tv, tbase + i + 1)
                inc(v.tensor_copy(out=dstp, in_=PTs[i % 2][:, 0:128]), "vv")
        return cnt["vv"]

    @block.gpsimd
    def _(g):
        dma(g, "gq", out=identb[:], in_=ident)

    @block.vector
    def _(v):
        v.memset(eps5[:], 1e-5)
        v.memset(eps8[:], 1e-8)

    barrier()

    with nc.sbuf_tensor("p0w", [128, 2, C], F32) as p0w, \
         nc.sbuf_tensor("p0s", [128, 2, C], BF16) as p0s, \
         nc.sbuf_tensor("p0st", [128, NI, 128], BF16) as p0st:

        ld_m, sg_m, tp_m, wr_m = {}, {}, {}, {}
        jobs = []
        for hc in range(8):
            jobs.append(("g", hc))
        for hc in range(8):
            jobs.append(("u", hc))
        for dt_i in range(16):
            jobs.append(("d", dt_i))

        for it, (kind, idx) in enumerate(jobs):
            @block.sync
            def _(sy):
                if it >= 2:
                    sy.wait_ge(ssc, sg_m[it - 2])
                if kind == "g":
                    src = Wgs[idx * 128:(idx + 1) * 128, :]
                elif kind == "u":
                    src = Wus[idx * 128:(idx + 1) * 128, :]
                else:
                    src = Wds[idx * 128:(idx + 1) * 128, :]
                w = C if kind in ("g", "u") else FH // 8
                ld_m[it] = dma(sy, "sq", out=p0w[:, it % 2, 0:w], in_=src)

            @block.scalar
            def _(sc):
                sc.wait_ge(sq_, ld_m[it])
                if it >= 2:
                    sc.wait_ge(tv, tp_m[it - 2])
                w = C if kind in ("g", "u") else FH // 8
                sg_m[it] = inc(sc.sign(out=p0s[:, it % 2, 0:w],
                                       in_=p0w[:, it % 2, 0:w]), "ssc")

            nblk = NI if kind in ("g", "u") else 8
            t_pre = [("ssc", sg_m[it])]
            if it >= 1:
                t_pre.append(("gq", wr_m[it - 1]))
            transpose_seq(
                [(p0s[:, it % 2, i * 128:i * 128 + 128], p0st[:, i, :])
                 for i in range(nblk)], t_pre=t_pre)
            tp_m[it] = cnt["tv"]
            cpd = cnt["vv"]

            @block.gpsimd
            def _(g):
                g.wait_ge(vv, cpd)
                if kind == "g":
                    wr_m[it] = dma(g, "gq", out=cin_g[idx].rearrange("i p c -> p i c"),
                                   in_=p0st[:])
                elif kind == "u":
                    wr_m[it] = dma(g, "gq", out=cin_u[idx].rearrange("i p c -> p i c"),
                                   in_=p0st[:])
                else:
                    wr_m[it] = dma(g, "gq",
                                   out=cin_d[:, :, idx * 128:(idx + 1) * 128].rearrange(
                                       "j p c -> p j c"),
                                   in_=p0st[:, 0:8, :])

    barrier()

    @block.gpsimd
    def _(g):
        g.wait_ge(gq, cnt["gq"])
        grp = [[0, 1, 2, 3, 4, 5, 6, 7]]
        inc(g.collective_compute("AllGather", ALU.bypass, ins=[cin_g], outs=[sgT_g],
                                 replica_groups=grp), "col")
        inc(g.collective_compute("AllGather", ALU.bypass, ins=[cin_u], outs=[sgT_u],
                                 replica_groups=grp), "col")
        inc(g.collective_compute("AllGather", ALU.bypass, ins=[cin_d], outs=[sgT_d],
                                 replica_groups=grp), "col")
    col_w = cnt["col"]

    ph_xqf = ExitStack()
    xqf = ph_xqf.enter_context(nc.sbuf_tensor("xqf", [128, NI, TOK], BF16))
    with nc.sbuf_tensor("pax", [128, 2, C], F32) as pax, \
         nc.sbuf_tensor("ps1", [128, C], F32) as ps1, \
         nc.sbuf_tensor("ps2", [128, C], F32) as ps2, \
         nc.sbuf_tensor("ps3", [128, C], F32) as ps3, \
         nc.sbuf_tensor("pai", [128, C], I32) as pai, \
         nc.sbuf_tensor("paxn", [128, 2, C], BF16) as paxn, \
         nc.sbuf_tensor("paxq", [128, 2, C], BF16) as paxq, \
         nc.sbuf_tensor("panf", [128, NI, 128], BF16) as panf, \
         nc.sbuf_tensor("psc", [128, 40], F32) as psc:

        x_m, v1_m, v2_m, v3_m, s1_m, s2_m, tq_m, w_m = {}, {}, {}, {}, {}, {}, {}, {}
        for c in range(NCH):
            @block.sync
            def _(sy):
                if c >= 2:
                    sy.wait_ge(vv, v3_m[c - 2])
                x_m[c] = dma(sy, "sq", out=pax[:, c % 2, :],
                                          in_=x_in[c * 128:(c + 1) * 128, :])

            @block.vector
            def _(v):
                xb = pax[:, c % 2, :]
                v.wait_ge(sq_, x_m[c])
                if c >= 2:
                    v.wait_ge(tv, tq_m[c - 2])
                m = psc[:, 0:1]; va = psc[:, 4:5]
                v.tensor_reduce(out=m, in_=xb, axis=AX.X, op=ALU.add)
                v.tensor_scalar_mul(m, m, 1.0 / C)
                v.tensor_scalar(out=ps1[:], in0=xb, scalar1=m, scalar2=None, op0=ALU.subtract)
                v.tensor_tensor(out=ps2[:], in0=ps1[:], in1=ps1[:], op=ALU.mult)
                v.tensor_reduce(out=va, in_=ps2[:], axis=AX.X, op=ALU.add)
                v1_m[c] = inc(v.tensor_scalar_mul(va, va, 1.0 / C), "vv")

            @block.scalar
            def _(sc):
                sc.wait_ge(vv, v1_m[c])
                s1_m[c] = inc(sc.activation(out=psc[:, 4:5], in_=psc[:, 4:5],
                                            func=AF.Sqrt, bias=eps5[:]), "ssc")

            @block.vector
            def _(v):
                v.wait_ge(ssc, s1_m[c])
                rstd = psc[:, 8:9]; m2 = psc[:, 12:13]; v2 = psc[:, 16:17]; am = psc[:, 20:21]
                nrecip(v, rstd, psc[:, 4:5], psc[:, 36:37])
                v.tensor_scalar(out=paxn[:, c % 2, :], in0=ps1[:], scalar1=rstd,
                                scalar2=None, op0=ALU.mult)
                v.tensor_scalar(out=ps3[:], in0=ps1[:], scalar1=rstd, scalar2=None, op0=ALU.mult)
                v.tensor_reduce(out=m2, in_=ps3[:], axis=AX.X, op=ALU.add)
                v.tensor_scalar_mul(m2, m2, 1.0 / C)
                v.tensor_scalar(out=ps1[:], in0=ps3[:], scalar1=m2, scalar2=None, op0=ALU.subtract)
                v.tensor_tensor(out=ps2[:], in0=ps1[:], in1=ps1[:], op=ALU.mult)
                v.tensor_reduce(out=v2, in_=ps2[:], axis=AX.X, op=ALU.add)
                v.tensor_scalar_mul(v2, v2, 1.0 / C)
                v2_m[c] = inc(v.tensor_reduce(out=am, in_=ps1[:], axis=AX.X, op=ALU.max,
                                              apply_absolute_value=True), "vv")

            @block.scalar
            def _(sc):
                sc.wait_ge(vv, v2_m[c])
                s2_m[c] = inc(sc.activation(out=psc[:, 16:17], in_=psc[:, 16:17],
                                            func=AF.Sqrt, bias=eps8[:]), "ssc")

            @block.vector
            def _(v):
                v.wait_ge(ssc, s2_m[c])
                am = psc[:, 20:21]; rq = psc[:, 24:25]; dsc = psc[:, 28:29]; t8 = psc[:, 32:33]
                nrecip(v, rq, am, psc[:, 36:37])
                v.tensor_scalar_mul(rq, rq, 127.0)
                v.tensor_scalar(out=ps2[:], in0=ps1[:], scalar1=rq, scalar2=None, op0=ALU.mult)
                v.tensor_scalar_add(ps2[:], ps2[:], 12582912.0)
                v.tensor_scalar_add(ps2[:], ps2[:], -12582912.0)
                nrecip(v, t8, psc[:, 16:17], psc[:, 36:37])
                v.tensor_tensor(out=dsc, in0=am, in1=t8, op=ALU.mult)
                v.tensor_scalar_mul(dsc, dsc, 1.0 / 127.0)
                v3_m[c] = inc(v.tensor_scalar(out=paxq[:, c % 2, :], in0=ps2[:], scalar1=dsc,
                                              scalar2=None, op0=ALU.mult), "vv")

            pairs = []
            for i in range(NI):
                pairs.append((paxn[:, c % 2, i * 128:i * 128 + 128], panf[:, i, :]))
                pairs.append((paxq[:, c % 2, i * 128:i * 128 + 128],
                              xqf[:, i, c * 128:(c + 1) * 128]))
            t_pre = [("vv", v3_m[c])]
            if c >= 1:
                t_pre.append(("gq", w_m[c - 1]))
            cpd = transpose_seq(pairs, t_pre=t_pre)
            tq_m[c] = cnt["tv"]

            @block.gpsimd
            def _(g):
                g.wait_ge(vv, cpd)
                w_m[c] = dma(g, "gq", out=xnsp[c].rearrange("i p c -> p i c"),
                                         in_=panf[:])

    if DEBUG_XQ:
        @block.gpsimd
        def _(g):
            for i in range(NI):
                dma(g, "gq", out=dbg_xq[i], in_=xqf[:, i, :])

    barrier()

    NHW = 32

    with nc.sbuf_tensor("wgs", [128, 2, 2, NI, 128], BF16) as wgs, \
         nc.sbuf_tensor("wus", [128, 2, 2, NI, 128], BF16) as wus, \
         nc.sbuf_tensor("wds", [128, 2, 2, C], BF16) as wds, \
         nc.sbuf_tensor("p_sb", [128, 2, 2, 1024], BF16) as p_sb, \
         nc.sbuf_tensor("gbf", [128, 2, 512], BF16) as gbf, \
         nc.sbuf_tensor("dacc", [128, 8, C], F32) as dacc, \
         nc.sbuf_tensor("gff", [128, 2, C], F32) as gff:

        wg_m, wd_m, p_m = {}, {}, {}
        dlast, gulast, evlast = {}, {}, {}
        dev_m, gffw_m, gffv_m = {}, {}, {}
        last_silu = [None]
        last_ptt = [None]
        all_rounds = [(tc, half) for tc in range(8) for half in range(2)]

        def emit_drounds(key, rounds):
            st, hw = key
            par = hw % 2
            for (tc, half) in rounds:
                dk_ = (st, tc, half)

                @block.tensor
                def _(t):
                    t.wait_ge(vv, p_m[key])
                    t.wait_ge(sq_, wd_m[key])
                    if half in evlast:
                        t.wait_ge(vv, evlast[half])
                    if hw == 0 and st == 1:
                        t.wait_ge(gq, gffw_m[0 * 8 + tc])
                    for hc2 in range(2):
                        for dw in range(2):
                            o = half * 1024 + dw * 512
                            inc(t.matmul(PD4[half * 2 + dw][:],
                                         p_sb[:, par, hc2, tc * 128:tc * 128 + 128],
                                         wds[:, par, hc2, o:o + 512],
                                         start=(hc2 == 0), stop=(hc2 == 1)), "tv")
                    dlast[key] = cnt["tv"]
                dmm = cnt["tv"]

                @block.vector
                def _(v):
                    v.wait_ge(tv, dmm)
                    for k in range(2):
                        dst = dacc[:, tc, half * 1024 + k * 512:half * 1024 + k * 512 + 512]
                        if hw == 0:
                            ii = v.tensor_copy(out=dst, in_=PD4[half * 2 + k][:])
                        else:
                            ii = v.tensor_tensor(out=dst, in0=PD4[half * 2 + k][:],
                                                 in1=dst, op=ALU.add)
                    dev_m[dk_] = inc(ii, "vv")
                    evlast[half] = dev_m[dk_]

        def emit_gff(stq):
            for tc in range(8):
                @block.scalar
                def _(sc):
                    for half in range(2):
                        sc.wait_ge(vv, dev_m[(stq, tc, half)])
                    if stq * 8 + tc >= 2:
                        sc.wait_ge(gq, gffw_m[stq * 8 + tc - 2])
                    gffv_m[stq * 8 + tc] = inc(
                        sc.activation(out=gff[:, tc % 2, :], in_=dacc[:, tc, :],
                                      func=AF.Silu), "ssc")

                @block.gpsimd
                def _(g):
                    g.wait_ge(ssc, gffv_m[stq * 8 + tc])
                    gffw_m[stq * 8 + tc] = dma(
                        g, "gq", out=gffn[stq * 8 + tc], in_=gff[:, tc % 2, :])

        for st in range(2):
            for hw in range(NHW):
                par = hw % 2
                hwg = st * NHW + hw
                key = (st, hw)
                pkey = None if hwg == 0 else ((st, hw - 1) if hw > 0 else (st - 1, NHW - 1))
                p2key = None
                if hwg >= 2:
                    p2key = (st, hw - 2) if hw >= 2 else (st - 1, NHW - 2 + hw)

                @block.sync
                def _(sy):
                    if hwg == 0:
                        sy.wait_ge(col, col_w)
                    if p2key is not None:
                        sy.wait_ge(tv, max(gulast[p2key], dlast[p2key]))
                    h0 = hw * 2
                    for d in range(2):
                        dma(sy, "sq", out=wgs[:, par, d, :, :],
                                         in_=sgT_g[h0 + d].rearrange("i p c -> p i c"))
                        dma(sy, "sq", out=wus[:, par, d, :, :],
                                         in_=sgT_u[h0 + d].rearrange("i p c -> p i c"))
                    wg_m[key] = cnt["sq"]
                    dma(sy, "sq", out=wds[:, par, :, :],
                                     in_=sgT_d[h0:h0 + 2].rearrange("h p c -> p h c"))
                    wd_m[key] = cnt["sq"]

                for gi, (hc, tw) in enumerate(((0, 0), (0, 1), (1, 0), (1, 1))):
                    sil_w = last_silu[0]
                    ptt_w = last_ptt[0]

                    @block.tensor
                    def _(t):
                        t.wait_ge(sq_, wg_m[key])
                        if sil_w is not None:
                            t.wait_ge(ssc, sil_w)
                        if ptt_w is not None:
                            t.wait_ge(vv, ptt_w)
                        for i in range(NI):
                            inc(t.matmul(PE1[:], wgs[:, par, hc, i, :],
                                         xqf[:, i, st * 1024 + tw * 512:st * 1024 + tw * 512 + 512],
                                         start=(i == 0), stop=(i == NI - 1)), "tv")
                        gm = cnt["tv"]
                        for i in range(NI):
                            inc(t.matmul(PE2[:], wus[:, par, hc, i, :],
                                         xqf[:, i, st * 1024 + tw * 512:st * 1024 + tw * 512 + 512],
                                         start=(i == 0), stop=(i == NI - 1)), "tv")
                        p_m[(key, gi, "g")] = gm
                        p_m[(key, gi, "u")] = cnt["tv"]
                    gulast[key] = cnt["tv"]

                    pt2 = last_ptt[0]

                    @block.scalar
                    def _(sc):
                        sc.wait_ge(tv, p_m[(key, gi, "g")])
                        if pt2 is not None:
                            sc.wait_ge(vv, pt2)
                        gffv = inc(sc.activation(out=gbf[:, gi % 2, :], in_=PE1[:],
                                                 func=AF.Silu), "ssc")
                        p_m[(key, gi, "s")] = gffv
                    last_silu[0] = cnt["ssc"]

                    @block.vector
                    def _(v):
                        v.wait_ge(tv, p_m[(key, gi, "u")])
                        v.wait_ge(ssc, p_m[(key, gi, "s")])
                        if p2key is not None and gi == 0:
                            v.wait_ge(tv, dlast[p2key])
                        pvv = inc(v.tensor_tensor(out=p_sb[:, par, hc, tw * 512:tw * 512 + 512],
                                                  in0=PE2[:], in1=gbf[:, gi % 2, :],
                                                  op=ALU.mult), "vv")
                        if gi == 3:
                            p_m[key] = pvv
                    last_ptt[0] = cnt["vv"]

                    if pkey is not None:
                        emit_drounds(pkey, all_rounds[gi * 4:(gi + 1) * 4])

                if st == 1 and hw == 0:
                    emit_gff(0)

        emit_drounds((1, NHW - 1), all_rounds)
        emit_gff(1)

    barrier()
    ph_xqf.close()

    ph_S = ExitStack()
    S_sb = ph_S.enter_context(nc.sbuf_tensor("S_sb", [128, C], F32))
    Sbf = ph_S.enter_context(nc.sbuf_tensor("Sbf", [128, C], BF16))
    cst = ph_S.enter_context(nc.sbuf_tensor("cst", [128, 6, C], F32))

    @block.sync
    def _(sy):
        dma(sy, "sq", out=cst[:, 0, :], in_=dq_t)
        dma(sy, "sq", out=cst[:, 1, :], in_=dk_t)
        dma(sy, "sq", out=cst[:, 2, :], in_=gl_t)
        dma(sy, "sq", out=cst[:, 3, :], in_=m2_t)
        dma(sy, "sq", out=cst[:, 4, :], in_=gnw_b)
        dma(sy, "sq", out=cst[:, 5, :], in_=gnb_b)

    @block.vector
    def _(v):
        inc(v.memset(S_sb[:], 0.0), "vv")

    with nc.sbuf_tensor("wbig", [128, NI, C], BF16) as wbig, \
         nc.sbuf_tensor("xnb4", [128, 2, NI, 128], BF16) as xnb4, \
         nc.sbuf_tensor("cosb", [128, C], F32) as cosb, \
         nc.sbuf_tensor("nsb", [128, C], F32) as nsb, \
         nc.sbuf_tensor("qa", [128, C], F32) as qa, \
         nc.sbuf_tensor("qb", [128, C], F32) as qb, \
         nc.sbuf_tensor("qout", [128, 2, C], BF16) as qout, \
         nc.sbuf_tensor("kb4", [128, 2, C], BF16) as kb4:

        def proj_pass(kind, wsrc_ap, dst):
            wld = {}

            @block.gpsimd
            def _(g):
                wld[0] = dma(g, "gq", out=wbig[:],
                                         in_=wsrc_ap.rearrange("(i p) c -> p i c", p=128))

            xn_m, tb_m, vw_m, sp_m, tab_m, kr_m, amm_m, sup_m = ({} for _ in range(8))
            for c in range(NCH):
                @block.sync
                def _(sy):
                    if c >= 2:
                        sy.wait_ge(tv, tb_m[(c - 2, 3)])
                    xn_m[c] = dma(sy, "sq", out=xnb4[:, c % 2, :, :],
                                               in_=xnsp[c].rearrange("i p c -> p i c"))
                    if kind in ("q", "k"):
                        if c >= 1:
                            sy.wait_ge(vv, vw_m[(c - 1, 3)])
                        dma(sy, "sq", out=cosb[:], in_=cos_t[c * 128:(c + 1) * 128, :])
                        dma(sy, "sq", out=nsb[:], in_=nsin_t[c * 128:(c + 1) * 128, :])
                        tab_m[c] = cnt["sq"]
                    if kind == "v":
                        if c >= 2:
                            sy.wait_ge(tv, amm_m[c - 2])
                        kr_m[c] = dma(sy, "sq", out=kb4[:, c % 2, :], in_=khat[c])

                fin_s = "ssc" if kind == "gate" else "vv"
                for w in range(4):
                    if w % 2 == 0:
                        pr = (w, w + 1)

                        @block.tensor
                        def _(t):
                            t.wait_ge(gq, wld[0])
                            t.wait_ge(sq_, xn_m[c])
                            if c >= 1:
                                t.wait_ge(sem[fin_s], vw_m[(c - 1, pr[1])])
                            for i in range(NI):
                                for w2 in pr:
                                    inc(t.matmul(PD4[w2][:],
                                                 xnb4[:, c % 2, i, :],
                                                 wbig[:, i, w2 * 512:w2 * 512 + 512],
                                                 start=(i == 0), stop=(i == NI - 1)),
                                        "tv")
                            tb_m[(c, pr[0])] = tb_m[(c, pr[1])] = cnt["tv"]

                    if kind in ("q", "k"):
                        @block.vector
                        def _(v):
                            v.wait_ge(tv, tb_m[(c, w)])
                            v.wait_ge(sq_, tab_m[c])
                            if c >= 2 and w == 0:
                                v.wait_ge(gq, sp_m[c - 2])
                                if kind == "k":
                                    pass
                            a = w * 512
                            sl = slice(a, a + 512)
                            v.tensor_tensor(out=qa[:, sl], in0=PD4[w][:], in1=cosb[:, sl],
                                            op=ALU.mult)
                            v.tensor_tensor(out=qb[:, a:a + 512:2],
                                            in0=PD4[w][:, 1:512:2],
                                            in1=nsb[:, a:a + 512:2], op=ALU.mult)
                            v.tensor_tensor(out=qb[:, a + 1:a + 512:2],
                                            in0=PD4[w][:, 0:512:2],
                                            in1=nsb[:, a + 1:a + 512:2], op=ALU.mult)
                            v.tensor_tensor(out=qa[:, sl], in0=qa[:, sl], in1=qb[:, sl],
                                            op=ALU.add)
                            dtab = cst[:, 0, sl] if kind == "q" else cst[:, 1, sl]
                            vw_m[(c, w)] = inc(v.tensor_tensor(out=qout[:, c % 2, sl],
                                                               in0=qa[:, sl], in1=dtab,
                                                               op=ALU.mult), "vv")
                    elif kind == "v":
                        @block.vector
                        def _(v):
                            v.wait_ge(tv, tb_m[(c, w)])
                            if c >= 2 and w == 0:
                                v.wait_ge(gq, sp_m[c - 2])
                                v.wait_ge(tv, amm_m[c - 1])
                            sl = slice(w * 512, w * 512 + 512)
                            vw_m[(c, w)] = inc(v.tensor_copy(out=qout[:, c % 2, sl],
                                                             in_=PD4[w][:]), "vv")
                    else:
                        @block.scalar
                        def _(sc):
                            sc.wait_ge(tv, tb_m[(c, w)])
                            if c >= 2 and w == 0:
                                sc.wait_ge(gq, sp_m[c - 2])
                            sl = slice(w * 512, w * 512 + 512)
                            vw_m[(c, w)] = inc(sc.activation(out=qout[:, c % 2, sl],
                                                             in_=PD4[w][:], func=AF.Silu),
                                               "ssc")

                @block.gpsimd
                def _(g):
                    g.wait_ge(sem[fin_s], vw_m[(c, 3)])
                    sp_m[c] = dma(g, "gq", out=dst[c], in_=qout[:, c % 2, :])

                if kind == "v":
                    for r in range(2):
                        @block.tensor
                        def _(t):
                            t.wait_ge(sq_, kr_m[c])
                            t.wait_ge(vv, vw_m[(c, 3)])
                            if r == 1:
                                t.wait_ge(vv, sup_m[(c, 0)])
                            elif c >= 1:
                                t.wait_ge(vv, sup_m[c - 1])
                            for hh in range(r * 8, r * 8 + 8):
                                pe = PE1 if (hh // 4) % 2 == 0 else PE2
                                inc(t.matmul(pe[:, (hh % 4) * 128:(hh % 4) * 128 + 128],
                                             kb4[:, c % 2, hh * 128:hh * 128 + 128],
                                             qout[:, c % 2, hh * 128:hh * 128 + 128],
                                             start=True, stop=True), "tv")
                            amm_m[(c, r)] = cnt["tv"]

                        @block.vector
                        def _(v):
                            v.wait_ge(tv, amm_m[(c, r)])
                            for gg in (r * 2, r * 2 + 1):
                                sl = slice(gg * 512, gg * 512 + 512)
                                pe = PE1 if gg % 2 == 0 else PE2
                                v.tensor_tensor(out=S_sb[:, sl], in0=S_sb[:, sl],
                                                in1=cst[:, 2, sl], op=ALU.mult)
                                inc(v.tensor_tensor(out=S_sb[:, sl], in0=S_sb[:, sl],
                                                    in1=pe[:], op=ALU.add), "vv")
                            sup_m[(c, r)] = cnt["vv"]
                    amm_m[c] = amm_m[(c, 1)]
                    sup_m[c] = sup_m[(c, 1)]
            return sp_m, sup_m

        proj_pass("q", wqkv[:, 0:C], qhat)
        barrier()
        proj_pass("k", wqkv[:, C:2 * C], khat)
        barrier()
        _, supv = proj_pass("v", wqkv[:, 2 * C:3 * C], vsp)

        @block.gpsimd
        def _(g):
            g.wait_ge(vv, supv[NCH - 1])
            dma(g, "gq", out=cin, in_=S_sb[:])
            g.wait_ge(gq, cnt["gq"])
            inc(g.collective_compute("AllGather", ALU.bypass, ins=[cin], outs=[cout],
                                     replica_groups=[[0, 1, 2, 3], [4, 5, 6, 7]]), "col")

        barrier()
        proj_pass("gate", wg_a, gsp)
        barrier()

    with nc.sbuf_tensor("agb", [128, 4, C], F32) as agb, \
         nc.sbuf_tensor("gmb", [128, 4, C], F32) as gmb:
        @block.sync
        def _(sy):
            sy.wait_ge(col, cnt["col"])
            dma(sy, "sq", out=agb[:], in_=cout.rearrange("(i p) c -> p i c", p=128))
            dma(sy, "sq", out=gmb[:], in_=gamA.rearrange("i p c -> p i c"))

        @block.vector
        def _(v):
            v.wait_ge(sq_, cnt["sq"])
            v.tensor_tensor(out=S_sb[:], in0=agb[:, 0, :], in1=gmb[:, 0, :], op=ALU.mult)
            for i in range(1, 4):
                v.tensor_tensor(out=agb[:, i, :], in0=agb[:, i, :], in1=gmb[:, i, :],
                                op=ALU.mult)
                v.tensor_tensor(out=S_sb[:], in0=S_sb[:], in1=agb[:, i, :], op=ALU.add)
            inc(v.tensor_copy(out=Sbf[:], in_=S_sb[:]), "vv")
    barrier()

    with nc.sbuf_tensor("qb4", [128, 2, C], BF16) as qb4, \
         nc.sbuf_tensor("kb4b", [128, 2, C], BF16) as kb4b, \
         nc.sbuf_tensor("vb4", [128, 2, C], BF16) as vb4, \
         nc.sbuf_tensor("gb4", [128, 2, C], BF16) as gb4, \
         nc.sbuf_tensor("qf", [128, H, 128], BF16) as qf, \
         nc.sbuf_tensor("kf", [128, H, 128], BF16) as kf, \
         nc.sbuf_tensor("scm", [128, C], BF16) as scm, \
         nc.sbuf_tensor("ynb", [128, C], F32) as ynb, \
         nc.sbuf_tensor("sqsb", [128, C], F32) as sqsb, \
         nc.sbuf_tensor("spb", [128, C], BF16) as spb, \
         nc.sbuf_tensor("spst", [128, NI, 128], BF16) as spst, \
         nc.sbuf_tensor("gnt", [128, 6, 16], F32) as gnt:

        ld_m, sc_m, ymm_m, amm_m, sup_m, gn_m, spw_m, tsp_m = ({} for _ in range(8))
        for c in range(NCH):
            @block.sync
            def _(sy):
                if c >= 2:
                    sy.wait_ge(tv, tsp_m[c - 2])
                dma(sy, "sq", out=qb4[:, c % 2, :], in_=qhat[c])
                dma(sy, "sq", out=kb4b[:, c % 2, :], in_=khat[c])
                dma(sy, "sq", out=vb4[:, c % 2, :], in_=vsp[c])
                dma(sy, "sq", out=gb4[:, c % 2, :], in_=gsp[c])
                ld_m[c] = cnt["sq"]

            pairs = []
            for hh in range(H):
                pairs.append((qb4[:, c % 2, hh * 128:hh * 128 + 128], qf[:, hh, :]))
                pairs.append((kb4b[:, c % 2, hh * 128:hh * 128 + 128], kf[:, hh, :]))
            t_pre = [("sq", ld_m[c])]
            cpd = transpose_seq(pairs, t_pre=t_pre)
            tr_cp = cpd

            for hh in range(H):
                @block.tensor
                def _(t):
                    t.wait_ge(vv, tr_cp)
                    pe = PE1 if hh % 2 == 0 else PE2
                    if hh >= 2:
                        t.wait_ge(vv, sc_m[(c, hh - 2)])
                    elif c >= 1:
                        t.wait_ge(vv, sup_m[c - 1])
                    inc(t.matmul(pe[:, 0:128], kf[:, hh, :], qf[:, hh, :],
                                 start=True, stop=True), "tv")
                    sc_m[(c, hh, "t")] = cnt["tv"]

                @block.vector
                def _(v):
                    v.wait_ge(tv, sc_m[(c, hh, "t")])
                    pe = PE1 if hh % 2 == 0 else PE2
                    sc_m[(c, hh)] = inc(
                        v.tensor_tensor(out=scm[:, hh * 128:hh * 128 + 128], in0=pe[:, 0:128],
                                        in1=cst[:, 3, hh * 128:hh * 128 + 128], op=ALU.mult),
                        "vv")

                @block.tensor
                def _(t):
                    t.wait_ge(vv, sc_m[(c, hh)])
                    if hh == 0 and c >= 1:
                        t.wait_ge(vv, gn_m[c - 1])
                    sl = slice(hh * 128, hh * 128 + 128)
                    psl = PD4[hh // 4][:, (hh % 4) * 128:(hh % 4) * 128 + 128]
                    inc(t.matmul(psl, qf[:, hh, :], Sbf[:, sl],
                                 start=True, stop=False), "tv")
                    inc(t.matmul(psl, scm[:, sl], vb4[:, c % 2, sl],
                                 start=False, stop=True), "tv")
                    ymm_m[(c, hh)] = cnt["tv"]

            for r in range(2):
                @block.tensor
                def _(t):
                    t.wait_ge(vv, sc_m[(c, H - 1)])
                    if r == 1:
                        t.wait_ge(vv, sup_m[(c, 0)])
                    for hh in range(r * 8, r * 8 + 8):
                        pe = PE1 if (hh // 4) % 2 == 0 else PE2
                        inc(t.matmul(pe[:, (hh % 4) * 128:(hh % 4) * 128 + 128],
                                     kb4b[:, c % 2, hh * 128:hh * 128 + 128],
                                     vb4[:, c % 2, hh * 128:hh * 128 + 128],
                                     start=True, stop=True), "tv")
                    amm_m[(c, r)] = cnt["tv"]

                @block.vector
                def _(v):
                    v.wait_ge(tv, amm_m[(c, r)])
                    if r == 0:
                        v.wait_ge(tv, ymm_m[(c, H - 1)])
                    for gg in (r * 2, r * 2 + 1):
                        sl = slice(gg * 512, gg * 512 + 512)
                        pe = PE1 if gg % 2 == 0 else PE2
                        v.tensor_tensor(out=S_sb[:, sl], in0=S_sb[:, sl],
                                        in1=cst[:, 2, sl], op=ALU.mult)
                        v.tensor_tensor(out=S_sb[:, sl], in0=S_sb[:, sl], in1=pe[:],
                                        op=ALU.add)
                        inc(v.tensor_copy(out=Sbf[:, sl], in_=S_sb[:, sl]), "vv")
                    sup_m[(c, r)] = cnt["vv"]
            sup_m[c] = sup_m[(c, 1)]

            @block.scalar
            def _(sc):
                sc.wait_ge(tv, ymm_m[(c, H - 1)])
                for j in range(4):
                    sc.activation(out=sqsb[:, j * 512:j * 512 + 512], in_=PD4[j][:],
                                  func=AF.Square)
                gn_m[(c, "sq")] = cnt["ssc"]

            @block.vector
            def _(v):
                v.wait_ge(tv, ymm_m[(c, H - 1)])
                mt = gnt[:, 0, :]; st16 = gnt[:, 1, :]; var16 = gnt[:, 2, :]
                for hh in range(H):
                    v.tensor_reduce(out=mt[:, hh:hh + 1],
                                    in_=PD4[hh // 4][:, (hh % 4) * 128:(hh % 4) * 128 + 128],
                                    axis=AX.X, op=ALU.add)
                v.tensor_scalar_mul(mt, mt, 1.0 / 128.0)
                v.wait_ge(ssc, gn_m[(c, "sq")])
                for hh in range(H):
                    v.tensor_reduce(out=st16[:, hh:hh + 1],
                                    in_=sqsb[:, hh * 128:hh * 128 + 128],
                                    axis=AX.X, op=ALU.add)
                v.tensor_scalar_mul(st16, st16, 1.0 / 128.0)
                v.tensor_tensor(out=var16, in0=mt, in1=mt, op=ALU.mult)
                gn_m[(c, "v")] = inc(v.tensor_tensor(out=var16, in0=st16, in1=var16,
                                                     op=ALU.subtract), "vv")

            @block.scalar
            def _(sc):
                sc.wait_ge(vv, gn_m[(c, "v")])
                gn_m[(c, "s")] = inc(sc.activation(out=gnt[:, 2, :], in_=gnt[:, 2, :],
                                                   func=AF.Sqrt, bias=eps5[:]), "ssc")

            @block.vector
            def _(v):
                v.wait_ge(ssc, gn_m[(c, "s")])
                mt = gnt[:, 0, :]; rst = gnt[:, 3, :]
                nrecip(v, rst, gnt[:, 2, :], gnt[:, 4, :])
                for hh in range(H):
                    v.tensor_scalar(out=ynb[:, hh * 128:hh * 128 + 128],
                                    in0=PD4[hh // 4][:, (hh % 4) * 128:(hh % 4) * 128 + 128],
                                    scalar1=mt[:, hh:hh + 1], scalar2=rst[:, hh:hh + 1],
                                    op0=ALU.subtract, op1=ALU.mult)
                v.tensor_tensor(out=ynb[:], in0=ynb[:], in1=cst[:, 4, :], op=ALU.mult)
                v.tensor_tensor(out=ynb[:], in0=ynb[:], in1=cst[:, 5, :], op=ALU.add)
                if c >= 1:
                    v.wait_ge(tv, tsp_m[c - 1])
                gn_m[c] = inc(v.tensor_tensor(out=spb[:], in0=ynb[:], in1=gb4[:, c % 2, :],
                                              op=ALU.mult), "vv")

            v_pre = [("gq", spw_m[c - 1])] if c >= 1 else None
            cpd2 = transpose_seq(
                [(spb[:, i * 128:i * 128 + 128], spst[:, i, :]) for i in range(NI)],
                t_pre=[("vv", gn_m[c])], v_pre=v_pre)
            tsp_m[c] = cnt["tv"]

            @block.gpsimd
            def _(g):
                g.wait_ge(vv, cpd2)
                spw_m[c] = dma(g, "gq", out=spf[c].rearrange("i p c -> p i c"),
                                           in_=spst[:])

    barrier()
    ph_S.close()

    with nc.sbuf_tensor("wpb", [128, NI, C], BF16) as wpb, \
         nc.sbuf_tensor("spfb", [128, 2, NI, 128], BF16) as spfb, \
         nc.sbuf_tensor("xb4", [128, 2, C], F32) as xb4, \
         nc.sbuf_tensor("gfb", [128, 2, C], F32) as gfb, \
         nc.sbuf_tensor("tmb", [128, C], F32) as tmb, \
         nc.sbuf_tensor("outb", [128, 2, C], F32) as outb:

        wld = {}

        @block.gpsimd
        def _(g):
            wld[0] = dma(g, "gq", out=wpb[:],
                                     in_=wproj.rearrange("(i p) c -> p i c", p=128))

        ld_m, mm_m, cb_m, ow_m = {}, {}, {}, {}
        for c in range(NCH):
            @block.sync
            def _(sy):
                if c >= 2:
                    sy.wait_ge(vv, cb_m[(c - 2, 3)])
                    sy.wait_ge(tv, mm_m[(c - 2, 3)])
                dma(sy, "sq", out=spfb[:, c % 2, :, :],
                                 in_=spf[c].rearrange("i p c -> p i c"))
                dma(sy, "sq", out=xb4[:, c % 2, :], in_=x_in[c * 128:(c + 1) * 128, :])
                dma(sy, "sq", out=gfb[:, c % 2, :], in_=gffn[c])
                ld_m[c] = cnt["sq"]

            for w in range(4):
                if w % 2 == 0:
                    pr = (w, w + 1)

                    @block.tensor
                    def _(t):
                        t.wait_ge(gq, wld[0])
                        t.wait_ge(sq_, ld_m[c])
                        if c >= 1:
                            t.wait_ge(vv, cb_m[(c - 1, pr[1])])
                        for i in range(NI):
                            for w2 in pr:
                                inc(t.matmul(PD4[w2][:],
                                             spfb[:, c % 2, i, :],
                                             wpb[:, i, w2 * 512:w2 * 512 + 512],
                                             start=(i == 0), stop=(i == NI - 1)), "tv")
                        mm_m[(c, pr[0])] = mm_m[(c, pr[1])] = cnt["tv"]

                @block.vector
                def _(v):
                    v.wait_ge(tv, mm_m[(c, w)])
                    if c >= 2 and w == 0:
                        v.wait_ge(gq, ow_m[c - 2])
                    sl = slice(w * 512, w * 512 + 512)
                    v.tensor_tensor(out=tmb[:, sl], in0=PD4[w][:], in1=gfb[:, c % 2, sl],
                                    op=ALU.mult)
                    cb_m[(c, w)] = inc(v.tensor_tensor(out=outb[:, c % 2, sl],
                                                       in0=tmb[:, sl],
                                                       in1=xb4[:, c % 2, sl], op=ALU.add),
                                       "vv")

            @block.gpsimd
            def _(g):
                g.wait_ge(vv, cb_m[(c, 3)])
                ow_m[c] = dma(g, "gq", out=out[c * 128:(c + 1) * 128, :],
                                          in_=outb[:, c % 2, :])

        @block.gpsimd
        def _(g):
            g.wait_ge(gq, ow_m[NCH - 1])

    es.close()
    return nc


def _tables(core):
    f8 = np.float64
    j = core % 4
    base = j * TOK
    i2 = np.arange(0, C, 2, dtype=f8)
    dim_scale = (i2 + 0.4 * C) / (1.4 * C)
    inv_freq = 1.0 / 10000.0 ** (i2 / C)
    pos = base + np.arange(TOK, dtype=f8)
    scale = dim_scale[None, :] ** (pos[:, None] / 512.0)
    sinu = pos[:, None] * inv_freq[None, :]
    sin = np.sin(sinu) * scale
    cos = np.cos(sinu) * scale
    cosf = np.empty((TOK, C), f8)
    cosf[:, 0::2] = cos
    cosf[:, 1::2] = cos
    nsinf = np.empty((TOK, C), f8)
    nsinf[:, 0::2] = -sin
    nsinf[:, 1::2] = sin

    gam = 1.0 - 2.0 ** (-5.0 - np.arange(H, dtype=f8))
    t_loc = np.arange(128, dtype=f8)
    dq = np.empty((128, C), f8)
    dk = np.empty((128, C), f8)
    m2 = np.zeros((128, C), f8)
    gl = np.empty((128, C), f8)
    tri = (t_loc[None, :] >= t_loc[:, None]).astype(f8)
    for h in range(H):
        sl = slice(h * 128, (h + 1) * 128)
        dq[:, sl] = (gam[h] ** t_loc)[:, None] * (128.0 ** -0.5)
        dk[:, sl] = (gam[h] ** (128.0 - t_loc))[:, None]
        m2[:, sl] = tri * gam[h] ** (-128.0)
        gl[:, sl] = gam[h] ** 128.0
    ga = np.zeros((4, 128, C), f8)
    for i in range(4):
        if i < j:
            for h in range(H):
                ga[i, :, h * 128:(h + 1) * 128] = gam[h] ** (2048.0 * (j - 1 - i))
    f = np.float32
    return (cosf.astype(f), nsinf.astype(f), dq.astype(f), dk.astype(f),
            m2.astype(f), gl.astype(f), ga.astype(f))


def kernel(L0_attn_x, L0_attn_w_qkv, L0_attn_w_g, L0_attn_w_proj,
           L0_attn_gn_weight, L0_attn_gn_bias, L0_ffn_W_g, L0_ffn_W_u, L0_ffn_W_d):
    x = np.ascontiguousarray(np.asarray(L0_attn_x, dtype=np.float32))
    B, T, _ = x.shape
    gnw = np.ascontiguousarray(np.broadcast_to(
        np.asarray(L0_attn_gn_weight, np.float32)[None, :], (128, C)))
    gnb = np.ascontiguousarray(np.broadcast_to(
        np.asarray(L0_attn_gn_bias, np.float32)[None, :], (128, C)))
    identm = np.eye(128, dtype=np.float32)
    wqkv = np.ascontiguousarray(np.asarray(L0_attn_w_qkv, np.float32))
    wg_a = np.ascontiguousarray(np.asarray(L0_attn_w_g, np.float32))
    wproj = np.ascontiguousarray(np.asarray(L0_attn_w_proj, np.float32))
    Wg = np.asarray(L0_ffn_W_g, np.float32)
    Wu = np.asarray(L0_ffn_W_u, np.float32)
    Wd = np.asarray(L0_ffn_W_d, np.float32)

    nc = build()
    in_maps = []
    for core in range(8):
        b = core // 4
        jj = core % 4
        cosf, nsinf, dq, dk, m2, gl, ga = _tables(core)
        xs = np.ascontiguousarray(x[b, jj * TOK:(jj + 1) * TOK, :])
        in_maps.append(dict(
            x=xs, wqkv=wqkv, wg_a=wg_a, wproj=wproj, gnw_b=gnw, gnb_b=gnb,
            Wgs=np.ascontiguousarray(Wg[core * 1024:(core + 1) * 1024, :]),
            Wus=np.ascontiguousarray(Wu[core * 1024:(core + 1) * 1024, :]),
            Wds=np.ascontiguousarray(Wd[:, core * 1024:(core + 1) * 1024]),
            cos_t=cosf, nsin_t=nsinf, dq_t=dq, dk_t=dk,
            m2_t=m2, gl_t=gl, gamA=ga, ident=identm))
    res = run_bass_kernel_spmd(nc, in_maps, core_ids=list(range(8)))
    outf = np.empty((B, T, C), np.float32)
    for core in range(8):
        b = core // 4
        jj = core % 4
        outf[b, jj * TOK:(jj + 1) * TOK, :] = res.results[core]["out"]
    return outf


# version 05 (final)
# speedup vs baseline: 1.7407x; 1.2594x over previous
import numpy as np
from contextlib import ExitStack

import concourse.bass as bass
import concourse.mybir as mybir
from concourse.bass_utils import run_bass_kernel_spmd

F32 = mybir.dt.float32
BF16 = mybir.dt.bfloat16
I32 = mybir.dt.int32
AF = mybir.ActivationFunctionType
ALU = mybir.AluOpType
AX = mybir.AxisListType

TOK = 2048
NCH = 16
DEBUG_XQ = False
SHARED = True
C = 2048
H = 16
FH = 8192
NI = 16


def build():
    nc = bass.Bass("TRN2", target_bir_lowering=False)

    def dri(n, s):
        return nc.dram_tensor(n, s, F32, kind="ExternalInput").ap()

    x_in = dri("x", [TOK, C])
    wqkv = dri("wqkv", [C, 3 * C])
    wg_a = dri("wg_a", [C, C])
    wproj = dri("wproj", [C, C])
    gnw_b = dri("gnw_b", [128, C])
    gnb_b = dri("gnb_b", [128, C])
    Wgs = dri("Wgs", [FH // 8, C])
    Wus = dri("Wus", [FH // 8, C])
    Wds = dri("Wds", [C, FH // 8])
    cos_t = dri("cos_t", [TOK, C])
    nsin_t = dri("nsin_t", [TOK, C])
    dq_t = dri("dq_t", [128, C])
    dk_t = dri("dk_t", [128, C])
    m2_t = dri("m2_t", [128, C])
    gl_t = dri("gl_t", [128, C])
    gamA = dri("gamA", [4, 128, C])
    ident = dri("ident", [128, 128])
    out = nc.dram_tensor("out", [TOK, C], F32, kind="ExternalOutput").ap()

    def idr(n, s, dt, space="Local"):
        return nc.dram_tensor(n, s, dt, kind="Internal", addr_space=space).ap()

    sgT_g = idr("sgT_g", [64, NI, 128, 128], BF16, "Shared" if SHARED else "Local")
    sgT_u = idr("sgT_u", [64, NI, 128, 128], BF16, "Shared" if SHARED else "Local")
    sgT_d = idr("sgT_d", [64, 128, C], BF16, "Shared" if SHARED else "Local")
    cin_g = idr("cin_g", [8, NI, 128, 128], BF16)
    cin_u = idr("cin_u", [8, NI, 128, 128], BF16)
    cin_d = idr("cin_d", [8, 128, C], BF16)
    xnsp = idr("xnsp", [NCH, NI, 128, 128], BF16)
    qhat = idr("qhat", [NCH, 128, C], BF16)
    khat = idr("khat", [NCH, 128, C], BF16)
    vsp = idr("vsp", [NCH, 128, C], BF16)
    gsp = idr("gsp", [NCH, 128, C], BF16)
    spf = idr("spf", [NCH, NI, 128, 128], BF16)
    gffn = idr("gffn", [NCH, 128, C], F32)
    dbg_xq = idr("dbg_xq", [NI, 128, TOK], BF16)
    dbg_dacc = idr("dbg_dacc", [NCH, 128, C], F32)
    cin = idr("cin", [128, C], F32)
    cout = idr("cout", [512, C], F32)

    es = ExitStack()
    E = es.enter_context
    identb = E(nc.sbuf_tensor("identb", [128, 128], BF16))
    eps5 = E(nc.sbuf_tensor("eps5", [128, 1], F32))
    eps8 = E(nc.sbuf_tensor("eps8", [128, 1], F32))
    PD4 = [E(nc.psum_tensor("PD%d" % j, [128, 512], F32)) for j in range(4)]
    PE1 = E(nc.psum_tensor("PE1", [128, 512], F32))
    PE2 = E(nc.psum_tensor("PE2", [128, 512], F32))
    PT1 = E(nc.psum_tensor("PT1", [128, 1024], BF16))
    PT2 = E(nc.psum_tensor("PT2", [128, 1024], BF16))
    PTs = (PT1, PT2)

    sq_ = E(nc.semaphore("sq"))
    gq = E(nc.semaphore("gq"))
    tv = E(nc.semaphore("tv"))
    vv = E(nc.semaphore("vv"))
    ssc = E(nc.semaphore("ssc"))
    bar = E(nc.semaphore("bar"))
    col = E(nc.semaphore("col"))
    _block_raw = E(nc.Block())

    cnt = {"sq": 0, "gq": 0, "tv": 0, "vv": 0, "ssc": 0, "col": 0}
    sem = {"sq": sq_, "gq": gq, "tv": tv, "vv": vv, "ssc": ssc, "col": col}

    _FENCED = {"tensor_tensor", "tensor_scalar", "tensor_scalar_mul",
               "tensor_scalar_add", "tensor_reduce", "tensor_copy",
               "reciprocal", "memset", "activation", "sign", "select"}

    class EngProxy:
        def __init__(self, eng, s):
            self._e, self._s = eng, s

        def __getattr__(self, nm):
            f = getattr(self._e, nm)
            if nm in _FENCED:
                def w(*a, **kw):
                    ii = f(*a, **kw)
                    ii.then_inc(sem[self._s], 1)
                    cnt[self._s] += 1
                    self._e.wait_ge(sem[self._s], cnt[self._s])
                    return ii
                return w
            return f

    class BlockProxy:
        def __init__(self, b):
            self._b = b

        def __getattr__(self, name):
            dec = getattr(self._b, name)
            if name == "vector":
                return lambda fn: dec(lambda eng: fn(EngProxy(eng, "vv")))
            if name == "scalar":
                return lambda fn: dec(lambda eng: fn(EngProxy(eng, "ssc")))
            return dec

    def inc(instr, s, n=None):
        if s in ("vv", "ssc"):
            return cnt[s]
        if n is None:
            n = 1
        instr.then_inc(sem[s], n)
        cnt[s] += n
        return cnt[s]

    block = BlockProxy(_block_raw)

    def nrecip(v, out, in_, tmp):
        v.reciprocal(out=out, in_=in_)
        v.tensor_tensor(out=tmp, in0=in_, in1=out, op=ALU.mult)
        v.tensor_scalar_mul(tmp, tmp, -1.0)
        v.tensor_scalar_add(tmp, tmp, 2.0)
        return v.tensor_tensor(out=out, in0=out, in1=tmp, op=ALU.mult)

    def dma(h, s, **kw):
        ii = h.dma_start(**kw)
        ii.then_inc(sem[s], 16)
        cnt[s] += 16
        h.wait_ge(sem[s], cnt[s])
        return cnt[s]

    ENGS = ("sync", "gpsimd", "scalar", "vector", "tensor")
    nbar = [0]

    def barrier():
        nbar[0] += 1
        tgt = nbar[0] * len(ENGS)
        for e in ENGS:
            @getattr(block, e)
            def _(h):
                h.sem_inc(bar, 1)
        for e in ENGS:
            @getattr(block, e)
            def _(h):
                h.wait_ge(bar, tgt)

    def transpose_seq(pairs, t_pre=None, v_pre=None):
        n = len(pairs)
        tbase = cnt["tv"]
        vbase = cnt["vv"]

        @block.tensor
        def _(t):
            if t_pre:
                for s, val in t_pre:
                    t.wait_ge(sem[s], val)
            for i, (src, dstp) in enumerate(pairs):
                if i >= 2:
                    t.wait_ge(vv, vbase + i - 1)
                else:
                    t.wait_ge(vv, vbase)
                inc(t.transpose(PTs[i % 2][:, 0:128], src, identb[:]), "tv")

        @block.vector
        def _(v):
            if v_pre:
                for s, val in v_pre:
                    v.wait_ge(sem[s], val)
            for i, (src, dstp) in enumerate(pairs):
                v.wait_ge(tv, tbase + i + 1)
                inc(v.tensor_copy(out=dstp, in_=PTs[i % 2][:, 0:128]), "vv")
        return cnt["vv"]

    @block.gpsimd
    def _(g):
        dma(g, "gq", out=identb[:], in_=ident)

    @block.vector
    def _(v):
        v.memset(eps5[:], 1e-5)
        v.memset(eps8[:], 1e-8)

    barrier()

    with nc.sbuf_tensor("p0w", [128, 2, C], F32) as p0w, \
         nc.sbuf_tensor("p0s", [128, 2, C], BF16) as p0s, \
         nc.sbuf_tensor("p0st", [128, NI, 128], BF16) as p0st:

        ld_m, sg_m, tp_m, wr_m = {}, {}, {}, {}
        jobs = []
        for hc in range(8):
            jobs.append(("g", hc))
        for hc in range(8):
            jobs.append(("u", hc))
        for dt_i in range(16):
            jobs.append(("d", dt_i))

        for it, (kind, idx) in enumerate(jobs):
            @block.sync
            def _(sy):
                if it >= 2:
                    sy.wait_ge(ssc, sg_m[it - 2])
                if kind == "g":
                    src = Wgs[idx * 128:(idx + 1) * 128, :]
                elif kind == "u":
                    src = Wus[idx * 128:(idx + 1) * 128, :]
                else:
                    src = Wds[idx * 128:(idx + 1) * 128, :]
                w = C if kind in ("g", "u") else FH // 8
                ld_m[it] = dma(sy, "sq", out=p0w[:, it % 2, 0:w], in_=src)

            @block.scalar
            def _(sc):
                sc.wait_ge(sq_, ld_m[it])
                if it >= 2:
                    sc.wait_ge(tv, tp_m[it - 2])
                w = C if kind in ("g", "u") else FH // 8
                sg_m[it] = inc(sc.sign(out=p0s[:, it % 2, 0:w],
                                       in_=p0w[:, it % 2, 0:w]), "ssc")

            nblk = NI if kind in ("g", "u") else 8
            t_pre = [("ssc", sg_m[it])]
            if it >= 1:
                t_pre.append(("gq", wr_m[it - 1]))
            transpose_seq(
                [(p0s[:, it % 2, i * 128:i * 128 + 128], p0st[:, i, :])
                 for i in range(nblk)], t_pre=t_pre)
            tp_m[it] = cnt["tv"]
            cpd = cnt["vv"]

            @block.gpsimd
            def _(g):
                g.wait_ge(vv, cpd)
                if kind == "g":
                    wr_m[it] = dma(g, "gq", out=cin_g[idx].rearrange("i p c -> p i c"),
                                   in_=p0st[:])
                elif kind == "u":
                    wr_m[it] = dma(g, "gq", out=cin_u[idx].rearrange("i p c -> p i c"),
                                   in_=p0st[:])
                else:
                    wr_m[it] = dma(g, "gq",
                                   out=cin_d[:, :, idx * 128:(idx + 1) * 128].rearrange(
                                       "j p c -> p j c"),
                                   in_=p0st[:, 0:8, :])

    barrier()

    @block.gpsimd
    def _(g):
        g.wait_ge(gq, cnt["gq"])
        grp = [[0, 1, 2, 3, 4, 5, 6, 7]]
        inc(g.collective_compute("AllGather", ALU.bypass, ins=[cin_g], outs=[sgT_g],
                                 replica_groups=grp), "col")
        inc(g.collective_compute("AllGather", ALU.bypass, ins=[cin_u], outs=[sgT_u],
                                 replica_groups=grp), "col")
        inc(g.collective_compute("AllGather", ALU.bypass, ins=[cin_d], outs=[sgT_d],
                                 replica_groups=grp), "col")
    col_w = cnt["col"]

    ph_xqf = ExitStack()
    xqf = ph_xqf.enter_context(nc.sbuf_tensor("xqf", [128, NI, TOK], BF16))
    with nc.sbuf_tensor("pax", [128, 2, C], F32) as pax, \
         nc.sbuf_tensor("ps1", [128, C], F32) as ps1, \
         nc.sbuf_tensor("ps2", [128, C], F32) as ps2, \
         nc.sbuf_tensor("ps3", [128, C], F32) as ps3, \
         nc.sbuf_tensor("pai", [128, C], I32) as pai, \
         nc.sbuf_tensor("paxn", [128, 2, C], BF16) as paxn, \
         nc.sbuf_tensor("paxq", [128, 2, C], BF16) as paxq, \
         nc.sbuf_tensor("panf", [128, NI, 128], BF16) as panf, \
         nc.sbuf_tensor("psc", [128, 40], F32) as psc:

        x_m, v1_m, v2_m, v3_m, s1_m, s2_m, tq_m, w_m = {}, {}, {}, {}, {}, {}, {}, {}
        for c in range(NCH):
            @block.sync
            def _(sy):
                if c >= 2:
                    sy.wait_ge(vv, v3_m[c - 2])
                x_m[c] = dma(sy, "sq", out=pax[:, c % 2, :],
                                          in_=x_in[c * 128:(c + 1) * 128, :])

            @block.vector
            def _(v):
                xb = pax[:, c % 2, :]
                v.wait_ge(sq_, x_m[c])
                if c >= 2:
                    v.wait_ge(tv, tq_m[c - 2])
                m = psc[:, 0:1]; va = psc[:, 4:5]
                v.tensor_reduce(out=m, in_=xb, axis=AX.X, op=ALU.add)
                v.tensor_scalar_mul(m, m, 1.0 / C)
                v.tensor_scalar(out=ps1[:], in0=xb, scalar1=m, scalar2=None, op0=ALU.subtract)
                v.tensor_tensor(out=ps2[:], in0=ps1[:], in1=ps1[:], op=ALU.mult)
                v.tensor_reduce(out=va, in_=ps2[:], axis=AX.X, op=ALU.add)
                v1_m[c] = inc(v.tensor_scalar_mul(va, va, 1.0 / C), "vv")

            @block.scalar
            def _(sc):
                sc.wait_ge(vv, v1_m[c])
                s1_m[c] = inc(sc.activation(out=psc[:, 4:5], in_=psc[:, 4:5],
                                            func=AF.Sqrt, bias=eps5[:]), "ssc")

            @block.vector
            def _(v):
                v.wait_ge(ssc, s1_m[c])
                rstd = psc[:, 8:9]; m2 = psc[:, 12:13]; v2 = psc[:, 16:17]; am = psc[:, 20:21]
                nrecip(v, rstd, psc[:, 4:5], psc[:, 36:37])
                v.tensor_scalar(out=paxn[:, c % 2, :], in0=ps1[:], scalar1=rstd,
                                scalar2=None, op0=ALU.mult)
                v.tensor_scalar(out=ps3[:], in0=ps1[:], scalar1=rstd, scalar2=None, op0=ALU.mult)
                v.tensor_reduce(out=m2, in_=ps3[:], axis=AX.X, op=ALU.add)
                v.tensor_scalar_mul(m2, m2, 1.0 / C)
                v.tensor_scalar(out=ps1[:], in0=ps3[:], scalar1=m2, scalar2=None, op0=ALU.subtract)
                v.tensor_tensor(out=ps2[:], in0=ps1[:], in1=ps1[:], op=ALU.mult)
                v.tensor_reduce(out=v2, in_=ps2[:], axis=AX.X, op=ALU.add)
                v.tensor_scalar_mul(v2, v2, 1.0 / C)
                v2_m[c] = inc(v.tensor_reduce(out=am, in_=ps1[:], axis=AX.X, op=ALU.max,
                                              apply_absolute_value=True), "vv")

            @block.scalar
            def _(sc):
                sc.wait_ge(vv, v2_m[c])
                s2_m[c] = inc(sc.activation(out=psc[:, 16:17], in_=psc[:, 16:17],
                                            func=AF.Sqrt, bias=eps8[:]), "ssc")

            @block.vector
            def _(v):
                v.wait_ge(ssc, s2_m[c])
                am = psc[:, 20:21]; rq = psc[:, 24:25]; dsc = psc[:, 28:29]; t8 = psc[:, 32:33]
                nrecip(v, rq, am, psc[:, 36:37])
                v.tensor_scalar_mul(rq, rq, 127.0)
                v.tensor_scalar(out=ps2[:], in0=ps1[:], scalar1=rq, scalar2=None, op0=ALU.mult)
                v.tensor_scalar_add(ps2[:], ps2[:], 12582912.0)
                v.tensor_scalar_add(ps2[:], ps2[:], -12582912.0)
                nrecip(v, t8, psc[:, 16:17], psc[:, 36:37])
                v.tensor_tensor(out=dsc, in0=am, in1=t8, op=ALU.mult)
                v.tensor_scalar_mul(dsc, dsc, 1.0 / 127.0)
                v3_m[c] = inc(v.tensor_scalar(out=paxq[:, c % 2, :], in0=ps2[:], scalar1=dsc,
                                              scalar2=None, op0=ALU.mult), "vv")

            pairs = []
            for i in range(NI):
                pairs.append((paxn[:, c % 2, i * 128:i * 128 + 128], panf[:, i, :]))
                pairs.append((paxq[:, c % 2, i * 128:i * 128 + 128],
                              xqf[:, i, c * 128:(c + 1) * 128]))
            t_pre = [("vv", v3_m[c])]
            if c >= 1:
                t_pre.append(("gq", w_m[c - 1]))
            cpd = transpose_seq(pairs, t_pre=t_pre)
            tq_m[c] = cnt["tv"]

            @block.gpsimd
            def _(g):
                g.wait_ge(vv, cpd)
                w_m[c] = dma(g, "gq", out=xnsp[c].rearrange("i p c -> p i c"),
                                         in_=panf[:])

    if DEBUG_XQ:
        @block.gpsimd
        def _(g):
            for i in range(NI):
                dma(g, "gq", out=dbg_xq[i], in_=xqf[:, i, :])

    barrier()

    NHW = 32

    with nc.sbuf_tensor("wgs", [128, 2, 2, NI, 128], BF16) as wgs, \
         nc.sbuf_tensor("wus", [128, 2, 2, NI, 128], BF16) as wus, \
         nc.sbuf_tensor("wds", [128, 4, 2048], BF16) as wds, \
         nc.sbuf_tensor("p_sb", [128, 4, 2, 1024], BF16) as p_sb, \
         nc.sbuf_tensor("gbf", [128, 2, 512], BF16) as gbf, \
         nc.sbuf_tensor("dacc", [128, 8, C], F32) as dacc, \
         nc.sbuf_tensor("gff", [128, C], F32) as gff:

        wg_m, wd_m, p_m = {}, {}, {}
        dlast, gulast, evlast = {}, {}, {}
        dev_m, gffw_m, gffv_m = {}, {}, {}
        last_silu = [None]
        last_ptt = [None]
        all_rounds = [(tc, half) for tc in range(8) for half in range(2)]

        def emit_drounds(p, rounds):
            st = p // 16
            first_of_st = p % 16 == 0
            for (tc, half) in rounds:
                dk_ = (st, tc, half)

                @block.tensor
                def _(t):
                    t.wait_ge(vv, max(p_m[2 * p], p_m[2 * p + 1]))
                    t.wait_ge(sq_, wd_m[p])
                    if half in evlast:
                        t.wait_ge(vv, evlast[half])
                    if first_of_st and st == 1:
                        t.wait_ge(gq, gffw_m[0 * 8 + tc])
                    for j4 in range(4):
                        par = (2 * p + j4 // 2) % 4
                        hc2 = j4 % 2
                        for dw in range(2):
                            o = half * 1024 + dw * 512
                            inc(t.matmul(PD4[half * 2 + dw][:],
                                         p_sb[:, par, hc2, tc * 128:tc * 128 + 128],
                                         wds[:, j4, o:o + 512],
                                         start=(j4 == 0), stop=(j4 == 3)), "tv")
                    dlast[p] = cnt["tv"]
                dmm = cnt["tv"]

                @block.vector
                def _(v):
                    v.wait_ge(tv, dmm)
                    for k in range(2):
                        dst = dacc[:, tc, half * 1024 + k * 512:half * 1024 + k * 512 + 512]
                        if first_of_st:
                            ii = v.tensor_copy(out=dst, in_=PD4[half * 2 + k][:])
                        else:
                            ii = v.tensor_tensor(out=dst, in0=PD4[half * 2 + k][:],
                                                 in1=dst, op=ALU.add)
                    dev_m[dk_] = inc(ii, "vv")
                    evlast[half] = dev_m[dk_]

        def emit_gff(stq):
            for tc in range(8):
                if DEBUG_XQ:
                    @block.gpsimd
                    def _(g):
                        for half in range(2):
                            g.wait_ge(vv, dev_m[(stq, tc, half)])
                        dma(g, "gq", out=dbg_dacc[stq * 8 + tc], in_=dacc[:, tc, :])

                @block.scalar
                def _(sc):
                    for half in range(2):
                        sc.wait_ge(vv, dev_m[(stq, tc, half)])
                    if stq * 8 + tc >= 1:
                        sc.wait_ge(gq, gffw_m[stq * 8 + tc - 1])
                    gffv_m[stq * 8 + tc] = inc(
                        sc.activation(out=gff[:], in_=dacc[:, tc, :],
                                      func=AF.Silu), "ssc")

                @block.gpsimd
                def _(g):
                    g.wait_ge(ssc, gffv_m[stq * 8 + tc])
                    gffw_m[stq * 8 + tc] = dma(g, "gq", out=gffn[stq * 8 + tc],
                                               in_=gff[:])

        for hwg in range(2 * NHW):
            st, hw = hwg // NHW, hwg % NHW
            par2 = hwg % 2
            par4 = hwg % 4
            p_act = (hwg - 2) // 2 if hwg >= 2 else None
            key = hwg

            @block.sync
            def _(sy):
                if hwg == 0:
                    sy.wait_ge(col, col_w)
                if hwg >= 2:
                    sy.wait_ge(tv, gulast[hwg - 2])
                h0 = (hwg % NHW) * 2
                for d in range(2):
                    dma(sy, "sq", out=wgs[:, par2, d, :, :],
                        in_=sgT_g[h0 + d].rearrange("i p c -> p i c"))
                    dma(sy, "sq", out=wus[:, par2, d, :, :],
                        in_=sgT_u[h0 + d].rearrange("i p c -> p i c"))
                wg_m[key] = cnt["sq"]
                if p_act is not None and hwg % 2 == 0:
                    if p_act >= 1:
                        sy.wait_ge(tv, dlast[p_act - 1])
                    hd = (p_act % 16) * 4
                    dma(sy, "sq", out=wds[:],
                        in_=sgT_d[hd:hd + 4].rearrange("h p c -> p h c"))
                    wd_m[p_act] = cnt["sq"]

            for gi, (hc, tw) in enumerate(((0, 0), (0, 1), (1, 0), (1, 1))):
                sil_w = last_silu[0]
                ptt_w = last_ptt[0]

                @block.tensor
                def _(t):
                    t.wait_ge(sq_, wg_m[key])
                    if sil_w is not None:
                        t.wait_ge(ssc, sil_w)
                    if ptt_w is not None:
                        t.wait_ge(vv, ptt_w)
                    for i in range(NI):
                        inc(t.matmul(PE1[:], wgs[:, par2, hc, i, :],
                                     xqf[:, i, st * 1024 + tw * 512:st * 1024 + tw * 512 + 512],
                                     start=(i == 0), stop=(i == NI - 1)), "tv")
                    gm = cnt["tv"]
                    for i in range(NI):
                        inc(t.matmul(PE2[:], wus[:, par2, hc, i, :],
                                     xqf[:, i, st * 1024 + tw * 512:st * 1024 + tw * 512 + 512],
                                     start=(i == 0), stop=(i == NI - 1)), "tv")
                    p_m[(key, gi, "g")] = gm
                    p_m[(key, gi, "u")] = cnt["tv"]
                gulast[key] = cnt["tv"]

                pt2 = last_ptt[0]

                @block.scalar
                def _(sc):
                    sc.wait_ge(tv, p_m[(key, gi, "g")])
                    if pt2 is not None:
                        sc.wait_ge(vv, pt2)
                    p_m[(key, gi, "s")] = inc(
                        sc.activation(out=gbf[:, gi % 2, :], in_=PE1[:],
                                      func=AF.Silu), "ssc")
                last_silu[0] = cnt["ssc"]

                @block.vector
                def _(v):
                    v.wait_ge(tv, p_m[(key, gi, "u")])
                    v.wait_ge(ssc, p_m[(key, gi, "s")])
                    if gi == 0 and hwg >= 4:
                        v.wait_ge(tv, dlast[(hwg - 4) // 2])
                    pvv = inc(v.tensor_tensor(out=p_sb[:, par4, hc, tw * 512:tw * 512 + 512],
                                              in0=PE2[:], in1=gbf[:, gi % 2, :],
                                              op=ALU.mult), "vv")
                    if gi == 3:
                        p_m[key] = pvv
                last_ptt[0] = cnt["vv"]

                if p_act is not None:
                    base = (hwg % 2) * 8 + gi * 2
                    emit_drounds(p_act, all_rounds[base:base + 2])

            if hwg == 33:
                emit_gff(0)


        @block.sync
        def _(sy):
            sy.wait_ge(tv, dlast[30])
            dma(sy, "sq", out=wds[:],
                in_=sgT_d[60:64].rearrange("h p c -> p h c"))
            wd_m[31] = cnt["sq"]

        emit_drounds(31, all_rounds)
        emit_gff(1)

    barrier()
    ph_xqf.close()

    ph_S = ExitStack()
    S_sb = ph_S.enter_context(nc.sbuf_tensor("S_sb", [128, C], F32))
    Sbf = ph_S.enter_context(nc.sbuf_tensor("Sbf", [128, C], BF16))
    cst = ph_S.enter_context(nc.sbuf_tensor("cst", [128, 6, C], F32))

    @block.sync
    def _(sy):
        dma(sy, "sq", out=cst[:, 0, :], in_=dq_t)
        dma(sy, "sq", out=cst[:, 1, :], in_=dk_t)
        dma(sy, "sq", out=cst[:, 2, :], in_=gl_t)
        dma(sy, "sq", out=cst[:, 3, :], in_=m2_t)
        dma(sy, "sq", out=cst[:, 4, :], in_=gnw_b)
        dma(sy, "sq", out=cst[:, 5, :], in_=gnb_b)

    @block.vector
    def _(v):
        inc(v.memset(S_sb[:], 0.0), "vv")

    with nc.sbuf_tensor("wbig", [128, NI, C], BF16) as wbig, \
         nc.sbuf_tensor("xnb4", [128, 2, NI, 128], BF16) as xnb4, \
         nc.sbuf_tensor("cosb", [128, C], F32) as cosb, \
         nc.sbuf_tensor("nsb", [128, C], F32) as nsb, \
         nc.sbuf_tensor("qa", [128, C], F32) as qa, \
         nc.sbuf_tensor("qb", [128, C], F32) as qb, \
         nc.sbuf_tensor("qout", [128, 2, C], BF16) as qout, \
         nc.sbuf_tensor("kb4", [128, 2, C], BF16) as kb4:

        def proj_pass(kind, wsrc_ap, dst):
            wld = {}

            @block.gpsimd
            def _(g):
                for q in range(4):
                    wld[q] = dma(g, "gq", out=wbig[:, q * 4:(q + 1) * 4, :],
                                 in_=wsrc_ap[q * 512:(q + 1) * 512, :].rearrange(
                                     "(i p) c -> p i c", p=128))

            xn_m, tb_m, vw_m, sp_m, tab_m, kr_m, amm_m, sup_m = ({} for _ in range(8))
            for c in range(NCH):
                @block.sync
                def _(sy):
                    if c >= 2:
                        sy.wait_ge(tv, tb_m[(c - 2, 3)])
                    xn_m[c] = dma(sy, "sq", out=xnb4[:, c % 2, :, :],
                                               in_=xnsp[c].rearrange("i p c -> p i c"))
                    if kind in ("q", "k"):
                        if c >= 1:
                            sy.wait_ge(vv, vw_m[(c - 1, 3)])
                        dma(sy, "sq", out=cosb[:], in_=cos_t[c * 128:(c + 1) * 128, :])
                        dma(sy, "sq", out=nsb[:], in_=nsin_t[c * 128:(c + 1) * 128, :])
                        tab_m[c] = cnt["sq"]
                    if kind == "v":
                        if c >= 2:
                            sy.wait_ge(tv, amm_m[c - 2])
                        kr_m[c] = dma(sy, "sq", out=kb4[:, c % 2, :], in_=khat[c])

                fin_s = "ssc" if kind == "gate" else "vv"
                for w in range(4):
                    if w % 2 == 0:
                        pr = (w, w + 1)

                        @block.tensor
                        def _(t):
                            t.wait_ge(sq_, xn_m[c])
                            if c >= 1:
                                t.wait_ge(sem[fin_s], vw_m[(c - 1, pr[1])])
                            for i in range(NI):
                                if i % 4 == 0 and c == 0:
                                    t.wait_ge(gq, wld[i // 4])
                                for w2 in pr:
                                    inc(t.matmul(PD4[w2][:],
                                                 xnb4[:, c % 2, i, :],
                                                 wbig[:, i, w2 * 512:w2 * 512 + 512],
                                                 start=(i == 0), stop=(i == NI - 1)),
                                        "tv")
                            tb_m[(c, pr[0])] = tb_m[(c, pr[1])] = cnt["tv"]

                    if kind in ("q", "k"):
                        @block.vector
                        def _(v):
                            v.wait_ge(tv, tb_m[(c, w)])
                            v.wait_ge(sq_, tab_m[c])
                            if c >= 2 and w == 0:
                                v.wait_ge(gq, sp_m[c - 2])
                                if kind == "k":
                                    pass
                            a = w * 512
                            sl = slice(a, a + 512)
                            v.tensor_tensor(out=qa[:, sl], in0=PD4[w][:], in1=cosb[:, sl],
                                            op=ALU.mult)
                            v.tensor_tensor(out=qb[:, a:a + 512:2],
                                            in0=PD4[w][:, 1:512:2],
                                            in1=nsb[:, a:a + 512:2], op=ALU.mult)
                            v.tensor_tensor(out=qb[:, a + 1:a + 512:2],
                                            in0=PD4[w][:, 0:512:2],
                                            in1=nsb[:, a + 1:a + 512:2], op=ALU.mult)
                            v.tensor_tensor(out=qa[:, sl], in0=qa[:, sl], in1=qb[:, sl],
                                            op=ALU.add)
                            dtab = cst[:, 0, sl] if kind == "q" else cst[:, 1, sl]
                            vw_m[(c, w)] = inc(v.tensor_tensor(out=qout[:, c % 2, sl],
                                                               in0=qa[:, sl], in1=dtab,
                                                               op=ALU.mult), "vv")
                    elif kind == "v":
                        @block.vector
                        def _(v):
                            v.wait_ge(tv, tb_m[(c, w)])
                            if c >= 2 and w == 0:
                                v.wait_ge(gq, sp_m[c - 2])
                                v.wait_ge(tv, amm_m[c - 1])
                            sl = slice(w * 512, w * 512 + 512)
                            vw_m[(c, w)] = inc(v.tensor_copy(out=qout[:, c % 2, sl],
                                                             in_=PD4[w][:]), "vv")
                    else:
                        @block.scalar
                        def _(sc):
                            sc.wait_ge(tv, tb_m[(c, w)])
                            if c >= 2 and w == 0:
                                sc.wait_ge(gq, sp_m[c - 2])
                            sl = slice(w * 512, w * 512 + 512)
                            vw_m[(c, w)] = inc(sc.activation(out=qout[:, c % 2, sl],
                                                             in_=PD4[w][:], func=AF.Silu),
                                               "ssc")

                @block.gpsimd
                def _(g):
                    g.wait_ge(sem[fin_s], vw_m[(c, 3)])
                    sp_m[c] = dma(g, "gq", out=dst[c], in_=qout[:, c % 2, :])

                if kind == "v":
                    for r in range(2):
                        @block.tensor
                        def _(t):
                            t.wait_ge(sq_, kr_m[c])
                            t.wait_ge(vv, vw_m[(c, 3)])
                            if r == 1:
                                t.wait_ge(vv, sup_m[(c, 0)])
                            elif c >= 1:
                                t.wait_ge(vv, sup_m[c - 1])
                            for hh in range(r * 8, r * 8 + 8):
                                pe = PE1 if (hh // 4) % 2 == 0 else PE2
                                inc(t.matmul(pe[:, (hh % 4) * 128:(hh % 4) * 128 + 128],
                                             kb4[:, c % 2, hh * 128:hh * 128 + 128],
                                             qout[:, c % 2, hh * 128:hh * 128 + 128],
                                             start=True, stop=True), "tv")
                            amm_m[(c, r)] = cnt["tv"]

                        @block.vector
                        def _(v):
                            v.wait_ge(tv, amm_m[(c, r)])
                            for gg in (r * 2, r * 2 + 1):
                                sl = slice(gg * 512, gg * 512 + 512)
                                pe = PE1 if gg % 2 == 0 else PE2
                                v.tensor_tensor(out=S_sb[:, sl], in0=S_sb[:, sl],
                                                in1=cst[:, 2, sl], op=ALU.mult)
                                inc(v.tensor_tensor(out=S_sb[:, sl], in0=S_sb[:, sl],
                                                    in1=pe[:], op=ALU.add), "vv")
                            sup_m[(c, r)] = cnt["vv"]
                    amm_m[c] = amm_m[(c, 1)]
                    sup_m[c] = sup_m[(c, 1)]
            return sp_m, sup_m

        proj_pass("q", wqkv[:, 0:C], qhat)
        barrier()
        proj_pass("k", wqkv[:, C:2 * C], khat)
        barrier()
        _, supv = proj_pass("v", wqkv[:, 2 * C:3 * C], vsp)

        @block.gpsimd
        def _(g):
            g.wait_ge(vv, supv[NCH - 1])
            dma(g, "gq", out=cin, in_=S_sb[:])
            g.wait_ge(gq, cnt["gq"])
            inc(g.collective_compute("AllGather", ALU.bypass, ins=[cin], outs=[cout],
                                     replica_groups=[[0, 1, 2, 3], [4, 5, 6, 7]]), "col")

        barrier()
        proj_pass("gate", wg_a, gsp)
        barrier()

    with nc.sbuf_tensor("agb", [128, 4, C], F32) as agb, \
         nc.sbuf_tensor("gmb", [128, 4, C], F32) as gmb:
        @block.sync
        def _(sy):
            sy.wait_ge(col, cnt["col"])
            dma(sy, "sq", out=agb[:], in_=cout.rearrange("(i p) c -> p i c", p=128))
            dma(sy, "sq", out=gmb[:], in_=gamA.rearrange("i p c -> p i c"))

        @block.vector
        def _(v):
            v.wait_ge(sq_, cnt["sq"])
            v.tensor_tensor(out=S_sb[:], in0=agb[:, 0, :], in1=gmb[:, 0, :], op=ALU.mult)
            for i in range(1, 4):
                v.tensor_tensor(out=agb[:, i, :], in0=agb[:, i, :], in1=gmb[:, i, :],
                                op=ALU.mult)
                v.tensor_tensor(out=S_sb[:], in0=S_sb[:], in1=agb[:, i, :], op=ALU.add)
            inc(v.tensor_copy(out=Sbf[:], in_=S_sb[:]), "vv")
    barrier()

    with nc.sbuf_tensor("qb4", [128, 2, C], BF16) as qb4, \
         nc.sbuf_tensor("kb4b", [128, 2, C], BF16) as kb4b, \
         nc.sbuf_tensor("vb4", [128, 2, C], BF16) as vb4, \
         nc.sbuf_tensor("gb4", [128, 2, C], BF16) as gb4, \
         nc.sbuf_tensor("qf", [128, H, 128], BF16) as qf, \
         nc.sbuf_tensor("kf", [128, H, 128], BF16) as kf, \
         nc.sbuf_tensor("scm", [128, C], BF16) as scm, \
         nc.sbuf_tensor("ynb", [128, C], F32) as ynb, \
         nc.sbuf_tensor("sqsb", [128, C], F32) as sqsb, \
         nc.sbuf_tensor("spb", [128, C], BF16) as spb, \
         nc.sbuf_tensor("spst", [128, NI, 128], BF16) as spst, \
         nc.sbuf_tensor("gnt", [128, 6, 16], F32) as gnt:

        ld_m, sc_m, ymm_m, amm_m, sup_m, gn_m, spw_m, tsp_m = ({} for _ in range(8))
        for c in range(NCH):
            @block.sync
            def _(sy):
                if c >= 2:
                    sy.wait_ge(tv, tsp_m[c - 2])
                dma(sy, "sq", out=qb4[:, c % 2, :], in_=qhat[c])
                dma(sy, "sq", out=kb4b[:, c % 2, :], in_=khat[c])
                dma(sy, "sq", out=vb4[:, c % 2, :], in_=vsp[c])
                dma(sy, "sq", out=gb4[:, c % 2, :], in_=gsp[c])
                ld_m[c] = cnt["sq"]

            pairs = []
            for hh in range(H):
                pairs.append((qb4[:, c % 2, hh * 128:hh * 128 + 128], qf[:, hh, :]))
                pairs.append((kb4b[:, c % 2, hh * 128:hh * 128 + 128], kf[:, hh, :]))
            t_pre = [("sq", ld_m[c])]
            cpd = transpose_seq(pairs, t_pre=t_pre)
            tr_cp = cpd

            for g4 in range(4):
                @block.tensor
                def _(t):
                    t.wait_ge(vv, tr_cp)
                    pe = PE1 if g4 % 2 == 0 else PE2
                    if g4 >= 2:
                        t.wait_ge(vv, sc_m[(c, g4 - 2)])
                    elif c >= 1:
                        t.wait_ge(vv, sup_m[c - 1])
                    for hh in range(g4 * 4, g4 * 4 + 4):
                        inc(t.matmul(pe[:, (hh % 4) * 128:(hh % 4) * 128 + 128],
                                     kf[:, hh, :], qf[:, hh, :],
                                     start=True, stop=True), "tv")
                    sc_m[(c, g4, "t")] = cnt["tv"]

                @block.vector
                def _(v):
                    v.wait_ge(tv, sc_m[(c, g4, "t")])
                    pe = PE1 if g4 % 2 == 0 else PE2
                    sc_m[(c, g4)] = inc(
                        v.tensor_tensor(out=scm[:, g4 * 512:g4 * 512 + 512], in0=pe[:],
                                        in1=cst[:, 3, g4 * 512:g4 * 512 + 512], op=ALU.mult),
                        "vv")

                @block.tensor
                def _(t):
                    t.wait_ge(vv, sc_m[(c, g4)])
                    if g4 == 0 and c >= 1:
                        t.wait_ge(vv, gn_m[c - 1])
                    for hh in range(g4 * 4, g4 * 4 + 4):
                        sl = slice(hh * 128, hh * 128 + 128)
                        psl = PD4[g4][:, (hh % 4) * 128:(hh % 4) * 128 + 128]
                        inc(t.matmul(psl, qf[:, hh, :], Sbf[:, sl],
                                     start=True, stop=False), "tv")
                        inc(t.matmul(psl, scm[:, sl], vb4[:, c % 2, sl],
                                     start=False, stop=True), "tv")
                    ymm_m[(c, g4)] = cnt["tv"]

            for r in range(2):
                @block.tensor
                def _(t):
                    t.wait_ge(vv, sc_m[(c, 3)])
                    if r == 1:
                        t.wait_ge(vv, sup_m[(c, 0)])
                    for hh in range(r * 8, r * 8 + 8):
                        pe = PE1 if (hh // 4) % 2 == 0 else PE2
                        inc(t.matmul(pe[:, (hh % 4) * 128:(hh % 4) * 128 + 128],
                                     kb4b[:, c % 2, hh * 128:hh * 128 + 128],
                                     vb4[:, c % 2, hh * 128:hh * 128 + 128],
                                     start=True, stop=True), "tv")
                    amm_m[(c, r)] = cnt["tv"]

                @block.vector
                def _(v):
                    v.wait_ge(tv, amm_m[(c, r)])
                    if r == 0:
                        v.wait_ge(tv, ymm_m[(c, 3)])
                    for gg in (r * 2, r * 2 + 1):
                        sl = slice(gg * 512, gg * 512 + 512)
                        pe = PE1 if gg % 2 == 0 else PE2
                        v.tensor_tensor(out=S_sb[:, sl], in0=S_sb[:, sl],
                                        in1=cst[:, 2, sl], op=ALU.mult)
                        v.tensor_tensor(out=S_sb[:, sl], in0=S_sb[:, sl], in1=pe[:],
                                        op=ALU.add)
                        inc(v.tensor_copy(out=Sbf[:, sl], in_=S_sb[:, sl]), "vv")
                    sup_m[(c, r)] = cnt["vv"]
            sup_m[c] = sup_m[(c, 1)]

            @block.scalar
            def _(sc):
                sc.wait_ge(tv, ymm_m[(c, 3)])
                for j in range(4):
                    sc.activation(out=sqsb[:, j * 512:j * 512 + 512], in_=PD4[j][:],
                                  func=AF.Square)
                gn_m[(c, "sq")] = cnt["ssc"]

            @block.vector
            def _(v):
                v.wait_ge(tv, ymm_m[(c, 3)])
                mt = gnt[:, 0, :]; st16 = gnt[:, 1, :]; var16 = gnt[:, 2, :]
                for hh in range(H):
                    v.tensor_reduce(out=mt[:, hh:hh + 1],
                                    in_=PD4[hh // 4][:, (hh % 4) * 128:(hh % 4) * 128 + 128],
                                    axis=AX.X, op=ALU.add)
                v.tensor_scalar_mul(mt, mt, 1.0 / 128.0)
                v.wait_ge(ssc, gn_m[(c, "sq")])
                for hh in range(H):
                    v.tensor_reduce(out=st16[:, hh:hh + 1],
                                    in_=sqsb[:, hh * 128:hh * 128 + 128],
                                    axis=AX.X, op=ALU.add)
                v.tensor_scalar_mul(st16, st16, 1.0 / 128.0)
                v.tensor_tensor(out=var16, in0=mt, in1=mt, op=ALU.mult)
                gn_m[(c, "v")] = inc(v.tensor_tensor(out=var16, in0=st16, in1=var16,
                                                     op=ALU.subtract), "vv")

            @block.scalar
            def _(sc):
                sc.wait_ge(vv, gn_m[(c, "v")])
                gn_m[(c, "s")] = inc(sc.activation(out=gnt[:, 2, :], in_=gnt[:, 2, :],
                                                   func=AF.Sqrt, bias=eps5[:]), "ssc")

            @block.vector
            def _(v):
                v.wait_ge(ssc, gn_m[(c, "s")])
                mt = gnt[:, 0, :]; rst = gnt[:, 3, :]
                nrecip(v, rst, gnt[:, 2, :], gnt[:, 4, :])
                for hh in range(H):
                    v.tensor_scalar(out=ynb[:, hh * 128:hh * 128 + 128],
                                    in0=PD4[hh // 4][:, (hh % 4) * 128:(hh % 4) * 128 + 128],
                                    scalar1=mt[:, hh:hh + 1], scalar2=rst[:, hh:hh + 1],
                                    op0=ALU.subtract, op1=ALU.mult)
                v.tensor_tensor(out=ynb[:], in0=ynb[:], in1=cst[:, 4, :], op=ALU.mult)
                v.tensor_tensor(out=ynb[:], in0=ynb[:], in1=cst[:, 5, :], op=ALU.add)
                if c >= 1:
                    v.wait_ge(tv, tsp_m[c - 1])
                gn_m[c] = inc(v.tensor_tensor(out=spb[:], in0=ynb[:], in1=gb4[:, c % 2, :],
                                              op=ALU.mult), "vv")

            v_pre = [("gq", spw_m[c - 1])] if c >= 1 else None
            cpd2 = transpose_seq(
                [(spb[:, i * 128:i * 128 + 128], spst[:, i, :]) for i in range(NI)],
                t_pre=[("vv", gn_m[c])], v_pre=v_pre)
            tsp_m[c] = cnt["tv"]

            @block.gpsimd
            def _(g):
                g.wait_ge(vv, cpd2)
                spw_m[c] = dma(g, "gq", out=spf[c].rearrange("i p c -> p i c"),
                                           in_=spst[:])

    barrier()
    ph_S.close()

    with nc.sbuf_tensor("wpb", [128, NI, C], BF16) as wpb, \
         nc.sbuf_tensor("spfb", [128, 2, NI, 128], BF16) as spfb, \
         nc.sbuf_tensor("xb4", [128, 2, C], F32) as xb4, \
         nc.sbuf_tensor("gfb", [128, 2, C], F32) as gfb, \
         nc.sbuf_tensor("tmb", [128, C], F32) as tmb, \
         nc.sbuf_tensor("outb", [128, 2, C], F32) as outb:

        wld = {}

        @block.gpsimd
        def _(g):
            wld[0] = dma(g, "gq", out=wpb[:],
                                     in_=wproj.rearrange("(i p) c -> p i c", p=128))

        ld_m, mm_m, cb_m, ow_m = {}, {}, {}, {}
        for c in range(NCH):
            @block.sync
            def _(sy):
                if c >= 2:
                    sy.wait_ge(vv, cb_m[(c - 2, 3)])
                    sy.wait_ge(tv, mm_m[(c - 2, 3)])
                dma(sy, "sq", out=spfb[:, c % 2, :, :],
                                 in_=spf[c].rearrange("i p c -> p i c"))
                dma(sy, "sq", out=xb4[:, c % 2, :], in_=x_in[c * 128:(c + 1) * 128, :])
                dma(sy, "sq", out=gfb[:, c % 2, :], in_=gffn[c])
                ld_m[c] = cnt["sq"]

            for w in range(4):
                if w % 2 == 0:
                    pr = (w, w + 1)

                    @block.tensor
                    def _(t):
                        t.wait_ge(gq, wld[0])
                        t.wait_ge(sq_, ld_m[c])
                        if c >= 1:
                            t.wait_ge(vv, cb_m[(c - 1, pr[1])])
                        for i in range(NI):
                            for w2 in pr:
                                inc(t.matmul(PD4[w2][:],
                                             spfb[:, c % 2, i, :],
                                             wpb[:, i, w2 * 512:w2 * 512 + 512],
                                             start=(i == 0), stop=(i == NI - 1)), "tv")
                        mm_m[(c, pr[0])] = mm_m[(c, pr[1])] = cnt["tv"]

                @block.vector
                def _(v):
                    v.wait_ge(tv, mm_m[(c, w)])
                    if c >= 2 and w == 0:
                        v.wait_ge(gq, ow_m[c - 2])
                    sl = slice(w * 512, w * 512 + 512)
                    v.tensor_tensor(out=tmb[:, sl], in0=PD4[w][:], in1=gfb[:, c % 2, sl],
                                    op=ALU.mult)
                    cb_m[(c, w)] = inc(v.tensor_tensor(out=outb[:, c % 2, sl],
                                                       in0=tmb[:, sl],
                                                       in1=xb4[:, c % 2, sl], op=ALU.add),
                                       "vv")

            @block.gpsimd
            def _(g):
                g.wait_ge(vv, cb_m[(c, 3)])
                ow_m[c] = dma(g, "gq", out=out[c * 128:(c + 1) * 128, :],
                                          in_=outb[:, c % 2, :])

        @block.gpsimd
        def _(g):
            g.wait_ge(gq, ow_m[NCH - 1])

    es.close()
    return nc


def _tables(core):
    f8 = np.float64
    j = core % 4
    base = j * TOK
    i2 = np.arange(0, C, 2, dtype=f8)
    dim_scale = (i2 + 0.4 * C) / (1.4 * C)
    inv_freq = 1.0 / 10000.0 ** (i2 / C)
    pos = base + np.arange(TOK, dtype=f8)
    scale = dim_scale[None, :] ** (pos[:, None] / 512.0)
    sinu = pos[:, None] * inv_freq[None, :]
    sin = np.sin(sinu) * scale
    cos = np.cos(sinu) * scale
    cosf = np.empty((TOK, C), f8)
    cosf[:, 0::2] = cos
    cosf[:, 1::2] = cos
    nsinf = np.empty((TOK, C), f8)
    nsinf[:, 0::2] = -sin
    nsinf[:, 1::2] = sin

    gam = 1.0 - 2.0 ** (-5.0 - np.arange(H, dtype=f8))
    t_loc = np.arange(128, dtype=f8)
    dq = np.empty((128, C), f8)
    dk = np.empty((128, C), f8)
    m2 = np.zeros((128, C), f8)
    gl = np.empty((128, C), f8)
    tri = (t_loc[None, :] >= t_loc[:, None]).astype(f8)
    for h in range(H):
        sl = slice(h * 128, (h + 1) * 128)
        dq[:, sl] = (gam[h] ** t_loc)[:, None] * (128.0 ** -0.5)
        dk[:, sl] = (gam[h] ** (128.0 - t_loc))[:, None]
        m2[:, sl] = tri * gam[h] ** (-128.0)
        gl[:, sl] = gam[h] ** 128.0
    ga = np.zeros((4, 128, C), f8)
    for i in range(4):
        if i < j:
            for h in range(H):
                ga[i, :, h * 128:(h + 1) * 128] = gam[h] ** (2048.0 * (j - 1 - i))
    f = np.float32
    return (cosf.astype(f), nsinf.astype(f), dq.astype(f), dk.astype(f),
            m2.astype(f), gl.astype(f), ga.astype(f))


def kernel(L0_attn_x, L0_attn_w_qkv, L0_attn_w_g, L0_attn_w_proj,
           L0_attn_gn_weight, L0_attn_gn_bias, L0_ffn_W_g, L0_ffn_W_u, L0_ffn_W_d):
    x = np.ascontiguousarray(np.asarray(L0_attn_x, dtype=np.float32))
    B, T, _ = x.shape
    gnw = np.ascontiguousarray(np.broadcast_to(
        np.asarray(L0_attn_gn_weight, np.float32)[None, :], (128, C)))
    gnb = np.ascontiguousarray(np.broadcast_to(
        np.asarray(L0_attn_gn_bias, np.float32)[None, :], (128, C)))
    identm = np.eye(128, dtype=np.float32)
    wqkv = np.ascontiguousarray(np.asarray(L0_attn_w_qkv, np.float32))
    wg_a = np.ascontiguousarray(np.asarray(L0_attn_w_g, np.float32))
    wproj = np.ascontiguousarray(np.asarray(L0_attn_w_proj, np.float32))
    Wg = np.asarray(L0_ffn_W_g, np.float32)
    Wu = np.asarray(L0_ffn_W_u, np.float32)
    Wd = np.asarray(L0_ffn_W_d, np.float32)

    nc = build()
    in_maps = []
    for core in range(8):
        b = core // 4
        jj = core % 4
        cosf, nsinf, dq, dk, m2, gl, ga = _tables(core)
        xs = np.ascontiguousarray(x[b, jj * TOK:(jj + 1) * TOK, :])
        in_maps.append(dict(
            x=xs, wqkv=wqkv, wg_a=wg_a, wproj=wproj, gnw_b=gnw, gnb_b=gnb,
            Wgs=np.ascontiguousarray(Wg[core * 1024:(core + 1) * 1024, :]),
            Wus=np.ascontiguousarray(Wu[core * 1024:(core + 1) * 1024, :]),
            Wds=np.ascontiguousarray(Wd[:, core * 1024:(core + 1) * 1024]),
            cos_t=cosf, nsin_t=nsinf, dq_t=dq, dk_t=dk,
            m2_t=m2, gl_t=gl, gamA=ga, ident=identm))
    res = run_bass_kernel_spmd(nc, in_maps, core_ids=list(range(8)))
    outf = np.empty((B, T, C), np.float32)
    for core in range(8):
        b = core // 4
        jj = core % 4
        outf[b, jj * TOK:(jj + 1) * TOK, :] = res.results[core]["out"]
    return outf
